# version 15
# baseline (speedup 1.0000x reference)
import numpy as np
import ml_dtypes
from contextlib import ExitStack

import concourse.bass as bass
import concourse.mybir as mybir
from concourse.bass_utils import run_bass_kernel_spmd
from concourse.bass_core_rust import IndirectOffsetOnAxis

F32 = mybir.dt.float32
BF16 = mybir.dt.bfloat16
I32 = mybir.dt.int32
U32 = mybir.dt.uint32
AF = mybir.ActivationFunctionType
ALU = mybir.AluOpType
AX = mybir.AxisListType
bfdt = ml_dtypes.bfloat16

S, D, ST = 4096, 1024, 32
NCORES = 8
NEXP, CAP = 8, 10240
DFF = 2048
CBLK = 1024
NBLK = NEXP + (2 * S) // CBLK
NCROWS = NBLK * CBLK


def configure(s, ncores=None, cap=None):
    global S, ST, NBLK, NCROWS, NCORES, CAP
    S, ST = s, s // 128
    NBLK = NEXP + (2 * S) // CBLK
    NCROWS = NBLK * CBLK
    if ncores is not None:
        NCORES = ncores
    if cap is not None:
        CAP = cap
NKV, NH, HDIM = 4, 16, 64
GQA_C = 32.0
QTL_ENG = "sync"
BIGPOS = 1.0e9

_HA = [0, 1, 2, 3, 8, 9, 10, 11]
_HB = [4, 5, 6, 7, 12, 13, 14, 15]
HEAD_PERM = []
for _j in range(8):
    HEAD_PERM += list(range(_HA[_j] * 64, _HA[_j] * 64 + 64))
    HEAD_PERM += list(range(_HB[_j] * 64, _HB[_j] * 64 + 64))
HEAD_PERM = np.array(HEAD_PERM)
HEAD_SLOT = {}
for _j in range(8):
    HEAD_SLOT[_HA[_j]] = (_j, 0)
    HEAD_SLOT[_HB[_j]] = (_j, 64)

ENGS = ("sync", "gpsimd", "vector", "scalar", "tensor")


class Sem:
    def __init__(self, nc, ctx, name):
        self.h = ctx.enter_context(nc.semaphore(name))
        self.v = 0


class B:
    def __init__(self, nc, ctx):
        self.nc = nc
        self.ctx = ctx
        self.ops = {e: [] for e in ENGS}
        self.sems = {}

    def sem(self, name):
        if name not in self.sems:
            self.sems[name] = Sem(self.nc, self.ctx, name)
        return self.sems[name]

    def emit(self, eng, fn):
        self.ops[eng].append(fn)

    def wait(self, eng, sem, val=None):
        v = sem.v if val is None else val
        if v > 0:
            self.emit(eng, lambda e, _s=sem.h, _v=v: e.wait_ge(_s, _v))

    def op(self, eng, fn, sem=None, n=1):
        if sem is not None:
            sem.v += n
            tgt = sem.v
            self.emit(eng, lambda e, _f=fn, _s=sem.h, _n=n: _f(e).then_inc(_s, _n))
            return tgt
        self.emit(eng, fn)
        return None

    def dma(self, eng, out, in_, sem, **kw):
        return self.op(eng, lambda e: e.dma_start(out=out, in_=in_, **kw), sem, 16)

    def sb(self, name, shape, dtype=F32, scope=None):
        return (scope or self.ctx).enter_context(self.nc.sbuf_tensor(name, shape, dtype))

    def ps(self, name, shape, dtype=F32, scope=None):
        return (scope or self.ctx).enter_context(self.nc.psum_tensor(name, shape, dtype))


def cdiv(a, b):
    return (a + b - 1) // b


def build_program(phases=("A", "B", "C"), taps=()):
    nc = bass.Bass("TRN2", target_bir_lowering=False, debug=False)
    ctx = ExitStack()
    b = B(nc, ctx)

    def inp(name, shape, dtype=F32):
        return nc.dram_tensor(name, shape, dtype, kind="ExternalInput").ap()

    x_in = inp("x", [S, D])
    wqT = inp("wqT", [D, D]); wkT = inp("wkT", [D, D]); wvT = inp("wvT", [D, D])
    qw = inp("qw", [D, D]); kw_ = inp("kw", [D, 256]); vw = inp("vw", [D, 256])
    ow = inp("ow", [D, D])
    qb = inp("qb", [128, 8]); kb = inp("kb", [128, 2]); vbr = inp("vbr", [1, 256])
    qbr = inp("qbr", [1, D]); kbr = inp("kbr", [1, 256])
    obr = inp("obr", [1, D])
    gw_in = inp("gw", [128, 64])
    w1_in = inp("w1", [NEXP * D, DFF]); w2_in = inp("w2", [NEXP * DFF, D])
    n1wb = inp("n1wb", [128, D]); n2wb = inp("n2wb", [128, D])
    identf = inp("identf", [128, 128])
    identb = inp("identb", [128, 128], BF16)
    triub = inp("triub", [128, 128], BF16)
    amask = inp("amask", [128, 128])
    onesrow = inp("onesrow", [1, 512], BF16)
    onesrow_f = inp("onesrow_f", [1, 128])
    onescol = inp("onescol", [128, 1], BF16)
    coremask = inp("coremask", [NCORES, 1])
    thr16 = inp("thr16", [1, 16])
    js24 = inp("js24", [1, NBLK])
    iota128 = inp("iota128", [128, 1])
    framp = inp("framp", [1, ST])

    out_ap = nc.dram_tensor("out", [S, D], F32, kind="ExternalOutput").ap()

    def dram(name, shape, dtype=F32):
        return nc.dram_tensor(name, shape, dtype, kind="Internal").ap()

    qsp = dram("qsp", [8, 128, S], BF16)
    x1h = dram("x1h", [S, D])
    s2h = dram("s2h", [S, D], BF16)
    w1b = dram("w1b", [NEXP * 128, 8 * DFF], BF16)
    w2b = dram("w2b", [NEXP * 128, 16 * D], BF16)
    cin = dram("cin", [NCROWS + 2 * S, D], BF16)
    cout = dram("cout", [NCROWS + 1, D], BF16)
    ag_in = dram("ag_in", [1, 8]); ag_out = dram("ag_out", [NCORES, 8])

    TAPS = {"x1": [S, D], "src2": [S, D], "probs": [128, 256],
            "route": [128, 256], "cnt": [1, 24]}
    tap_map = {t: nc.dram_tensor("tap_" + t, TAPS[t], F32, kind="ExternalOutput").ap()
               for t in taps}

    c_idb = b.sb("c_idb", [128, 128], BF16)

    sLD = b.sem("sLD"); sGW = b.sem("sGW"); sPH = b.sem("sPH")
    sDX = b.sem("sDX"); sVA = b.sem("sVA"); sSA = b.sem("sSA")
    sPE = b.sem("sPE"); sQS = b.sem("sQS"); sX1 = b.sem("sX1")
    sGP = b.sem("sGP"); sCC = b.sem("sCC"); sOUT = b.sem("sOUT")

    sLDg = b.sem("sLDg")
    b.dma("gpsimd", c_idb[:], identb, sLDg)

    c_sc = b.sb("c_sc", [128, 8])
    for col, val in [(0, 1.0 / D), (1, 1e-5), (2, 1e-6), (3, -0.5), (4, -GQA_C),
                     (5, 1.0 / 32.0), (6, 1.0 / 8.0)]:
        b.op("vector", lambda v, col=col, val=val: v.memset(c_sc[:, col:col + 1], val), sVA)
    CONSTV = sVA.v
    for e in ("scalar", "tensor", "gpsimd", "sync"):
        b.wait(e, sVA, CONSTV)
    CINVD = c_sc[:, 0:1]; CEPS5 = c_sc[:, 1:2]; CEPS6 = c_sc[:, 2:3]
    CNEGH = c_sc[:, 3:4]; CNEGC = c_sc[:, 4:5]; CINV32 = c_sc[:, 5:6]
    CINV8 = c_sc[:, 6:7]
    for e in ENGS:
        b.wait(e, sLD, sLD.v)
        b.wait(e, sLDg, sLDg.v)

    if "C" in phases:
        for e in range(NEXP):
            b.dma("gpsimd",
                  w1b[e * 128:(e + 1) * 128, :].rearrange("p (c h) -> p c h", c=8),
                  w1_in[e * D:(e + 1) * D, :].rearrange("(c p) h -> p c h", p=128), sGW)
            b.dma("gpsimd",
                  w2b[e * 128:(e + 1) * 128, :].rearrange("p (c h) -> p c h", c=16),
                  w2_in[e * DFF:(e + 1) * DFF, :].rearrange("(c p) h -> p c h", p=128), sGW)
        zb = b.sb("zb", [128, D], BF16)
        b.op("gpsimd", lambda g: g.memset(zb[:], 0.0), sGP)
        b.wait("gpsimd", sGP, sGP.v)
        for r in range(NCROWS // 128):
            b.dma("gpsimd", cin[r * 128:(r + 1) * 128, :], zb[:], sGW)
        b.dma("gpsimd", cout[NCROWS:NCROWS + 1, :], zb[0:1, :], sGW)
    GW_TOTAL = sGW.v

    def emit_ln(xsrc, xc, sqscr, stat, mcol, rcol, ln_out, eps_ap, ln_prev_done):
        b.op("vector", lambda v: v.reduce_sum(stat[:, 0:1], xsrc, axis=AX.X), sVA)
        b.wait("vector", sVA)
        b.op("vector", lambda v: v.tensor_scalar_mul(mcol, stat[:, 0:1], 1.0 / D), sVA)
        b.wait("vector", sVA)
        b.op("vector", lambda v: v.tensor_scalar(xc, xsrc, mcol, None, op0=ALU.subtract), sVA)
        xc_t = sVA.v
        b.wait("scalar", sVA, xc_t)
        b.op("scalar", lambda s: s.activation(sqscr, xc, AF.Square, accum_out=stat[:, 1:2]), sSA)
        b.wait("scalar", sSA)
        b.op("scalar", lambda s: s.activation(stat[:, 2:3], stat[:, 1:2], AF.Ln, scale=CINVD, bias=eps_ap), sSA)
        b.wait("scalar", sSA)
        b.op("scalar", lambda s: s.activation(rcol, stat[:, 2:3], AF.Exp, scale=CNEGH), sSA)
        rstd_t = sSA.v
        b.wait("vector", sSA, rstd_t)
        ln_t = b.op("vector", lambda v: v.tensor_scalar_mul(ln_out, xc, rcol), sVA)
        return ln_t, xc_t

    if "A" in phases:
        phA = ExitStack()
        kfm = b.sb("kfm", [128, 8 * S], BF16, phA)
        vtm = b.sb("vtm", [128, ST * D], BF16, phA)

        p1 = ExitStack()
        xbuf = b.sb("xbuf", [128, 2 * D], F32, p1)
        st1 = b.sb("st1", [128, 4], F32, p1)
        m1s = b.sb("m1s", [128, ST], F32, p1)
        r1s = b.sb("r1s", [128, ST], F32, p1)
        lntm = b.sb("lntm", [128, 2 * D], BF16, p1)
        lnf = b.sb("lnf", [128, 2 * 4096], BF16, p1)
        qwin = b.sb("qwin", [128, 512], BF16, p1)
        wq_s = b.sb("wq_s", [128, 8 * D], BF16, p1)
        wk_s = b.sb("wk_s", [128, 8 * D], BF16, p1)
        wv_s = b.sb("wv_s", [128, 8 * D], BF16, p1)
        ptA = b.ps("ptA", [128, 128], BF16, p1)
        ptB = b.ps("ptB", [128, 128], BF16, p1)
        pj0 = b.ps("pj0", [128, 512], F32, p1)
        pj1 = b.ps("pj1", [128, 512], F32, p1)
        pj2 = b.ps("pj2", [128, 512], F32, p1)
        pj3 = b.ps("pj3", [128, 512], F32, p1)

        for dst, src in [(wq_s, wqT), (wk_s, wkT), (wv_s, wvT)]:
            for c in range(8):
                b.dma("gpsimd", dst[:, c * D:(c + 1) * D],
                      src[c * 128:(c + 1) * 128, :], sLDg)
        for e in ENGS:
            b.wait(e, sLD, sLD.v)
            b.wait(e, sLDg, sLDg.v)

        tr_evac = []
        xfree = {}
        xload = []
        sDXa = (b.sem("sDXa0"), b.sem("sDXa1"))
        qspill_mark = [0]
        aspills = []
        tr_last = {}
        pend_w = [None]

        def emit_q(i4):
            lbase = (i4 % 2) * 4096
            b.wait("tensor", sVA, tr_last[i4])
            ev_q = None
            for oc in range(8):
                if ev_q is not None:
                    b.wait("tensor", sSA, ev_q)
                for c in range(8):
                    mm = lambda t, oc=oc, c=c, lbase=lbase: t.matmul(
                        pj0[:, :], wq_s[:, c * D + oc * 128: c * D + oc * 128 + 128],
                        lnf[:, lbase + c * 512: lbase + (c + 1) * 512],
                        start=(c == 0), stop=(c == 7))
                    tq = b.op("tensor", mm, sPE if c == 7 else None)
                b.wait("scalar", sPE, tq)
                b.wait("scalar", sQS, sQS.v)
                ev_q = b.op("scalar", lambda s, oc=oc: s.activation(
                    qwin[:, 0:512], pj0[:, :], AF.Copy,
                    scale=CINV32), sSA)
                b.wait("sync", sSA, ev_q)
                b.dma("sync", qsp[oc, :, i4 * 512:(i4 + 1) * 512],
                      qwin[:, 0:512], sQS)
            qspill_mark[0] = sQS.v
            aspills.append(qspill_mark[0])

        def emit_k(i4):
            lbase = (i4 % 2) * 4096
            ev_k = None
            for oc in range(8):
                if ev_k is not None:
                    b.wait("tensor", sSA, ev_k)
                for c in range(8):
                    mm = lambda t, oc=oc, c=c, lbase=lbase: t.matmul(
                        pj1[:, :], wk_s[:, c * D + oc * 128: c * D + oc * 128 + 128],
                        lnf[:, lbase + c * 512: lbase + (c + 1) * 512],
                        start=(c == 0), stop=(c == 7))
                    tk = b.op("tensor", mm, sPE if c == 7 else None)
                b.wait("scalar", sPE, tk)
                ev_k = b.op("scalar", lambda s, oc=oc, i4=i4: s.activation(
                    kfm[:, oc * S + i4 * 512: oc * S + i4 * 512 + 512], pj1[:, :], AF.Copy), sSA)

        def emit_v(i4):
            lbase = (i4 % 2) * 4096
            ev_v = {0: None, 1: None}
            for sc in range(4):
                st_idx = i4 * 4 + sc
                for half, pj in ((0, pj2), (1, pj3)):
                    if ev_v[half] is not None:
                        b.wait("tensor", sVA, ev_v[half])
                    for c in range(8):
                        mm = lambda t, sc=sc, half=half, pj=pj, c=c, lbase=lbase: t.matmul(
                            pj[:, :],
                            lnf[:, lbase + c * 512 + sc * 128: lbase + c * 512 + sc * 128 + 128],
                            wv_s[:, c * D + half * 512: c * D + half * 512 + 512],
                            start=(c == 0), stop=(c == 7))
                        tv = b.op("tensor", mm, sPE if c == 7 else None)
                    b.wait("vector", sPE, tv)
                    ev_v[half] = b.op("vector", lambda v, st_idx=st_idx, half=half, pj=pj: v.tensor_copy(
                        vtm[:, st_idx * D + half * 512: st_idx * D + half * 512 + 512], pj[:, :]), sVA)

        tr_done = {}
        for i in range(ST):
            slot = i % 2
            wi = i // 4
            w = i % 4
            xap = xbuf[:, slot * D:(slot + 1) * D]
            if i >= 2:
                b.wait("sync", sVA, xfree[i - 2])
            xload.append(b.dma("sync", xap, x_in[i * 128:(i + 1) * 128, :], sDXa[slot]))
            b.wait("vector", sDXa[slot], xload[i])
            if i >= 1:
                b.wait("vector", sSA)
            if i >= 2:
                b.wait("scalar", sPE, tr_done[i - 2])
            lslot = lntm[:, (i % 2) * D:(i % 2 + 1) * D]
            ln_t, xc_t = emit_ln(xap, xap, lslot, st1, m1s[:, i:i + 1],
                                 r1s[:, i:i + 1], lslot, CEPS5, None)
            xfree[i] = ln_t
            if pend_w[0] is not None:
                if w == 0:
                    emit_q(pend_w[0])
                elif w == 1:
                    emit_k(pend_w[0])
                elif w == 2:
                    emit_v(pend_w[0])
                    pend_w[0] = None
            b.wait("tensor", sVA, ln_t)
            for c in range(8):
                pt = (ptA, ptB)[c % 2]
                if len(tr_evac) >= 2:
                    b.wait("tensor", sVA, tr_evac[-2])
                tp = b.op("tensor", lambda t, pt=pt, c=c, i=i: t.transpose(
                    pt[:, :], lntm[:, (i % 2) * D + c * 128: (i % 2) * D + (c + 1) * 128],
                    c_idb[:]), sPE)
                b.wait("vector", sPE, tp)
                tr_evac.append(b.op("vector", lambda v, pt=pt, c=c, w=w, wi=wi: v.tensor_copy(
                    lnf[:, (wi % 2) * 4096 + c * 512 + w * 128:
                        (wi % 2) * 4096 + c * 512 + w * 128 + 128], pt[:, :]), sVA))
            tr_done[i] = sPE.v
            if w == 3:
                tr_last[wi] = tr_evac[-1]
                pend_w[0] = wi
        if pend_w[0] is not None:
            emit_q(pend_w[0])
            emit_k(pend_w[0])
            emit_v(pend_w[0])
            pend_w[0] = None
        KFM_DONE = sSA.v
        VTM_DONE = sVA.v
        p1.close()

        if "A1" in phases:
            for e in ("tensor", "sync"):
                b.wait(e, sSA, KFM_DONE)
                b.wait(e, sVA, VTM_DONE)
                b.wait(e, sQS)
            sDV1 = b.sem("sDV1")
            dv1 = ExitStack()
            tb1 = b.sb("tb1", [128, D], F32, dv1)
            for i in range(ST):
                t = b.dma("sync", tb1[:], x_in[i * 128:(i + 1) * 128, :], sDV1)
                b.wait("sync", sDV1, t)
                st = b.dma("sync", out_ap[i * 128:(i + 1) * 128, :], tb1[:], sOUT)
                b.wait("sync", sOUT, st)
            dv1.close()

        if "A1" in phases:
            phA.close()
            with nc.Block() as block:
                @block.sync
                def _(e):
                    for f in b.ops["sync"]:
                        f(e)

                @block.gpsimd
                def _(e):
                    for f in b.ops["gpsimd"]:
                        f(e)

                @block.vector
                def _(e):
                    for f in b.ops["vector"]:
                        f(e)

                @block.scalar
                def _(e):
                    for f in b.ops["scalar"]:
                        f(e)

                @block.tensor
                def _(e):
                    for f in b.ops["tensor"]:
                        f(e)
            ctx.close()
            return nc
        p2 = ExitStack()
        for e in ("tensor", "sync"):
            b.wait(e, sSA, KFM_DONE)
            b.wait(e, sVA, VTM_DONE)
            b.wait(e, sQS)
        qtl = b.sb("qtl", [128, 3 * 1024], BF16, p2)
        xres = b.sb("xres", [128, 2 * D], F32, p2)
        srow = b.sb("srow", [128, S], F32, p2)
        prow = b.sb("prow", [128, 2 * S], BF16, p2)
        ptsb = b.sb("ptsb", [128, 3 * 128], BF16, p2)
        x1t = b.sb("x1t", [128, 2 * D], F32, p2)
        st2 = b.sb("st2", [128, 8], F32, p2)
        c_amask = b.sb("c_amask", [128, 128], F32, p2)
        sAM = b.sem("sAM")
        b.dma("sync", c_amask[:], amask, sAM)
        b.wait("vector", sAM, sAM.v)
        psc0 = b.ps("psc0", [128, 512], F32, p2)
        psc1 = b.ps("psc1", [128, 512], F32, p2)
        psc2 = b.ps("psc2", [128, 512], F32, p2)
        ppt0 = b.ps("ppt0", [128, 128], BF16, p2)
        ppt1 = b.ps("ppt1", [128, 128], BF16, p2)
        ppt2 = b.ps("ppt2", [128, 128], BF16, p2)
        ppv0 = b.ps("ppv0", [128, 512], F32, p2)
        ppv1 = b.ps("ppv1", [128, 512], F32, p2)

        sDXb = (b.sem("sDXb0"), b.sem("sDXb1"))
        sDXg = (b.sem("sDXg0"), b.sem("sDXg1"), b.sem("sDXg2"))
        sX1p = (b.sem("sX1p0"), b.sem("sX1p1"))
        bank_free = {0: 0, 1: 0, 2: 0}
        x1_done = {}
        sc_done = {}
        tr_marks = {}
        exp_m = {}
        rl_m = {}
        qld_m = {}
        xld_m = {}
        pendA = [None]

        def emit_qld(ti):
            if ti >= ST:
                return
            sl = ti % 3
            if ti >= 3:
                b.wait(QTL_ENG, sPE, sc_done[ti - 3])
            b.wait(QTL_ENG, sQS, aspills[ti // 4])
            for oc in range(8):
                qld_m[ti] = b.dma(QTL_ENG,
                                  qtl[:, sl * 1024 + oc * 128: sl * 1024 + oc * 128 + 128],
                                  qsp[oc, :, ti * 128:(ti + 1) * 128], sDXg[sl])

        def emit_tailA(ti):
            slot = ti % 2
            psl = ti % 2
            sb4 = 4 * (ti % 2)
            b.wait("tensor", sSA, exp_m[ti])
            if ti > 0:
                b.wait("tensor", sVA, x1_done[ti - 1])
            ptev = []

            def emit_pv(j, ti=ti):
                b.wait("tensor", sVA, ptev[j])
                for half, ppv in ((0, ppv0), (1, ppv1)):
                    mm = lambda t, j=j, half=half, ppv=ppv, ti=ti: t.matmul(
                        ppv[:, :], ptsb[:, (j % 3) * 128:(j % 3) * 128 + 128],
                        vtm[:, j * D + half * 512: j * D + half * 512 + 512],
                        start=(j == 0), stop=(j == ti))
                    return_t = b.op("tensor", mm, sPE if (j == ti and half == 1) else None)
                return return_t

            for j in range(ti + 1):
                pt = (ppt0, ppt1, ppt2)[j % 3]
                if j >= 3:
                    b.wait("tensor", sVA, ptev[j - 3])
                tp = b.op("tensor", lambda t, j=j, pt=pt, psl=psl: t.transpose(
                    pt[:, :], prow[:, psl * S + j * 128: psl * S + (j + 1) * 128],
                    c_idb[:]), sPE)
                b.wait("vector", sPE, tp)
                ptev.append(b.op("vector", lambda v, j=j, pt=pt: v.tensor_copy(
                    ptsb[:, (j % 3) * 128:(j % 3) * 128 + 128], pt[:, :]), sVA))
                if j >= 1:
                    emit_pv(j - 1)
            pv_t = emit_pv(ti)
            tr_marks[ti] = sPE.v
            b.wait("vector", sPE, pv_t)
            b.wait("vector", sVA, rl_m[ti])
            b.wait("vector", sDXb[slot], xld_m[ti])
            b.wait("vector", sX1p[slot], sX1p[slot].v)
            for half, ppv in ((0, ppv0), (1, ppv1)):
                x1_done[ti] = b.op("vector", lambda v, slot=slot, half=half, ppv=ppv,
                                   sb4=sb4: v.scalar_tensor_tensor(
                    x1t[:, slot * D + half * 512: slot * D + half * 512 + 512],
                    ppv[:, :], st2[:, sb4 + 3:sb4 + 4],
                    xres[:, slot * D + half * 512: slot * D + half * 512 + 512],
                    op0=ALU.mult, op1=ALU.add), sVA)
            b.wait("sync", sVA, x1_done[ti])
            b.dma("sync", x1h[ti * 128:(ti + 1) * 128, :],
                  x1t[:, slot * D:(slot + 1) * D], sX1p[slot])

        emit_qld(0)
        emit_qld(1)
        for qi in range(ST):
            slot = qi % 2
            qsl3 = qi % 3
            sb4 = 4 * (qi % 2)
            emit_qld(qi + 2)
            if qi >= 2:
                b.wait("sync", sVA, x1_done[qi - 2])
            xld_m[qi] = b.dma("sync", xres[:, slot * D:(slot + 1) * D],
                              x_in[qi * 128:(qi + 1) * 128, :], sDXb[slot])
            KW = cdiv((qi + 1) * 128, 512)
            Wq = (qi + 1) * 128
            if "P2Z" in phases:
                b.wait("vector", sDXb[slot], xld_m[qi])
                b.wait("vector", sX1p[slot], sX1p[slot].v)
                for half in range(2):
                    x1_done[qi] = b.op("vector", lambda v, slot=slot, half=half: v.tensor_copy(
                        x1t[:, slot * D + half * 512: slot * D + half * 512 + 512],
                        xres[:, slot * D + half * 512: slot * D + half * 512 + 512]), sVA)
                b.wait("sync", sVA, x1_done[qi])
                b.dma("sync", x1h[qi * 128:(qi + 1) * 128, :], x1t[:, slot * D:(slot + 1) * D], sX1p[slot])
                continue
            b.wait("tensor", sDXg[qsl3], qld_m[qi])
            if qi >= 1:
                b.wait("scalar", sSA, exp_m[qi - 1])
            if qi >= 2:
                b.wait("vector", sVA, x1_done[qi - 2])
            for kwi in range(KW):
                bk = kwi % 3
                psc = (psc0, psc1, psc2)[bk]
                b.wait("tensor", sSA, bank_free[bk])
                for c in range(8):
                    mm = lambda t, kwi=kwi, c=c, qsl3=qsl3, psc=psc: t.matmul(
                        psc[:, :], qtl[:, qsl3 * 1024 + c * 128: qsl3 * 1024 + c * 128 + 128],
                        kfm[:, c * S + kwi * 512: c * S + kwi * 512 + 512],
                        start=(c == 0), stop=(c == 7))
                    tm = b.op("tensor", mm, sPE if c == 7 else None)
                wlim = min(512, Wq - kwi * 512)
                b.wait("scalar", sPE, tm)
                ev = b.op("scalar", lambda s, kwi=kwi, psc=psc, wlim=wlim: s.activation(
                    srow[:, kwi * 512: kwi * 512 + wlim], psc[:, 0:wlim], AF.Copy), sSA)
                bank_free[bk] = ev
                b.wait("vector", sSA, ev)
                b.op("vector", lambda v, kwi=kwi, wlim=wlim, sb4=sb4: v.reduce_max(
                    st2[:, sb4 + 3:sb4 + 4], srow[:, kwi * 512: kwi * 512 + wlim],
                    axis=AX.X), sVA)
                b.wait("vector", sVA)
                if kwi == 0:
                    b.op("vector", lambda v, sb4=sb4: v.tensor_copy(
                        st2[:, sb4:sb4 + 1], st2[:, sb4 + 3:sb4 + 4]), sVA)
                else:
                    b.op("vector", lambda v, sb4=sb4: v.tensor_max(
                        st2[:, sb4:sb4 + 1], st2[:, sb4:sb4 + 1], st2[:, sb4 + 3:sb4 + 4]), sVA)
            sc_done[qi] = sPE.v
            b.wait("vector", sSA)
            b.op("vector", lambda v, qi=qi: v.tensor_add(
                srow[:, qi * 128: qi * 128 + 128], srow[:, qi * 128: qi * 128 + 128],
                c_amask[:]), sVA)
            b.wait("vector", sVA)
            b.op("vector", lambda v, sb4=sb4: v.tensor_scalar_mul(
                st2[:, sb4 + 1:sb4 + 2], st2[:, sb4:sb4 + 1], -1.0), sVA)
            nrm_t = sVA.v
            b.wait("scalar", sVA, nrm_t)
            if qi >= 2:
                b.wait("scalar", sPE, tr_marks[qi - 2])
            b.op("scalar", lambda s, Wq=Wq, qi=qi, sb4=sb4: s.activation(
                prow[:, (qi % 2) * S: (qi % 2) * S + Wq], srow[:, 0:Wq], AF.Exp,
                bias=st2[:, sb4 + 1:sb4 + 2], accum_out=st2[:, sb4 + 2:sb4 + 3]), sSA)
            exp_m[qi] = sSA.v
            b.wait("vector", sSA, exp_m[qi])
            b.op("vector", lambda v, sb4=sb4: v.reciprocal(
                st2[:, sb4 + 3:sb4 + 4], st2[:, sb4 + 2:sb4 + 3]), sVA)
            rl_m[qi] = sVA.v
            if pendA[0] is not None:
                emit_tailA(pendA[0])
            pendA[0] = qi
        if pendA[0] is not None and "P2Z" not in phases:
            emit_tailA(pendA[0])
        p2.close()
        phA.close()

        X1_ALL = (sX1p[0], sX1p[0].v, sX1p[1], sX1p[1].v)
        if "x1" in tap_map:
            b.wait("sync", sX1p[0], sX1p[0].v)
            b.wait("sync", sX1p[1], sX1p[1].v)
            tb = b.sb("tapb", [128, D])
            for i in range(ST):
                t = b.dma("sync", tb[:], x1h[i * 128:(i + 1) * 128, :], sDX)
                b.wait("sync", sDX, t)
                b.dma("sync", tap_map["x1"][i * 128:(i + 1) * 128, :], tb[:], sX1)

        b.wait("sync", sX1p[0], X1_ALL[1])
        b.wait("sync", sX1p[1], X1_ALL[3])
        b.op("sync", lambda e: e.nop(), sPH)
        for e in ("vector", "scalar", "tensor", "gpsimd"):
            b.wait(e, sPH, sPH.v)

    if "B" in phases:
        phB = ExitStack()
        c_idf = b.sb("c_idf", [128, 128], F32, phB)
        c_triu = b.sb("c_triu", [128, 128], BF16, phB)
        c_or = b.sb("c_or", [1, 512], BF16, phB)
        c_qbr = b.sb("c_qbr", [1, D], BF16, phB)
        c_kbr = b.sb("c_kbr", [1, 256], BF16, phB)
        c_orf = b.sb("c_orf", [1, 128], F32, phB)
        c_oc = b.sb("c_oc", [128, 1], BF16, phB)
        c_cm = b.sb("c_cm", [NCORES, 1], F32, phB)
        c_thr = b.sb("c_thr", [1, 16], F32, phB)
        c_js = b.sb("c_js", [1, NBLK], F32, phB)
        c_iota = b.sb("c_iota", [128, 1], F32, phB)
        c_qb = b.sb("c_qb", [128, 8], F32, phB)
        c_kb = b.sb("c_kb", [128, 2], F32, phB)
        c_vbr = b.sb("c_vbr", [1, 256], BF16, phB)
        c_obr = b.sb("c_obr", [1, D], BF16, phB)
        c_gw = b.sb("c_gw", [128, 64], F32, phB)
        c_n1 = b.sb("c_n1", [128, D], F32, phB)
        c_n2 = b.sb("c_n2", [128, D], F32, phB)
        m2s = b.sb("m2s", [128, ST], F32, phB)
        r2s = b.sb("r2s", [128, ST], F32, phB)
        logits = b.sb("logits", [128, ST * 8], F32, phB)
        probs = b.sb("probs", [128, ST * 8], F32, phB)
        w0c = b.sb("w0c", [128, ST], F32, phB)
        w1c = b.sb("w1c", [128, ST], F32, phB)
        wofff = b.sb("wofff", [1, NBLK], F32, phB)
        woffi = b.sb("woffi", [1, NBLK], I32, phB)
        wobase = b.sb("wobase", [1, 2 * NBLK], F32, phB)
        widxf = b.sb("widxf", [128, NBLK], F32, phB)
        widxu = b.sb("widxu", [128, NBLK], U32, phB)
        wtmp = b.sb("wtmp", [128, NBLK], F32, phB)
        sc0u = b.sb("sc0u", [128, ST], U32, phB)
        sc1u = b.sb("sc1u", [128, ST], U32, phB)
        ssc0 = b.sb("ssc0", [128, ST], U32, phB)
        ssc1 = b.sb("ssc1", [128, ST], U32, phB)
        c_framp = b.sb("c_framp", [1, ST], F32, phB)
        fbb = b.sb("fbb", [128, ST], F32, phB)
        sLDB = b.sem("sLDB")
        for dst, src in [(c_idf, identf), (c_cm, coremask), (c_thr, thr16),
                         (c_js, js24), (c_qb, qb), (c_kb, kb), (c_gw, gw_in),
                         (c_n1, n1wb), (c_n2, n2wb), (c_orf, onesrow_f),
                         (c_iota, iota128), (c_framp, framp)]:
            b.dma("sync", dst[:], src, sLDB)
        sLDBg = b.sem("sLDBg")
        for dst, src in [(c_triu, triub), (c_or, onesrow), (c_oc, onescol),
                         (c_vbr, vbr), (c_obr, obr), (c_qbr, qbr), (c_kbr, kbr)]:
            b.dma("gpsimd", dst[:], src, sLDBg)

        pATT = ExitStack()
        attc = b.sb("attc", [128, ST * D], BF16, pATT)
        pQKV = ExitStack()
        k2f = b.sb("k2f", [128, 4 * S], BF16, pQKV)
        v2t = b.sb("v2t", [128, ST * 260], BF16, pQKV)

        pB1 = ExitStack()
        x1b = b.sb("x1b", [128, 2 * D], F32, pB1)
        st3 = b.sb("st3", [128, 4], F32, pB1)
        ln2tm = b.sb("ln2tm", [128, D], BF16, pB1)
        lnf2 = b.sb("lnf2", [128, 8 * 512], BF16, pB1)
        qw_s = b.sb("qw_s2", [128, 8 * D], BF16, pB1)
        qwin2 = b.sb("qwin2", [128, 8 * 512], BF16, pB1)
        kw_s = b.sb("kw_s2", [128, 8 * 256], BF16, pB1)
        vw_s = b.sb("vw_s2", [128, 8 * 256], BF16, pB1)
        ptA2 = b.ps("ptA2", [128, 128], BF16, pB1)
        ptB2 = b.ps("ptB2", [128, 128], BF16, pB1)
        pq0 = b.ps("pq0", [128, 512], F32, pB1)
        pq1 = b.ps("pq1", [128, 512], F32, pB1)
        pq2 = b.ps("pq2", [128, 256], F32, pB1)
        for c in range(8):
            b.dma("gpsimd", qw_s[:, c * D:(c + 1) * D], qw[c * 128:(c + 1) * 128, :], sLDBg)
            b.dma("gpsimd", kw_s[:, c * 256:(c + 1) * 256], kw_[c * 128:(c + 1) * 128, :], sLDBg)
            b.dma("gpsimd", vw_s[:, c * 256:(c + 1) * 256], vw[c * 128:(c + 1) * 128, :], sLDBg)
        for e in ENGS:
            b.wait(e, sLDB, sLDB.v)
            b.wait(e, sLDBg, sLDBg.v)
        vones = b.op("vector", lambda v: v.memset(
            bass.AP(v2t.ap().tensor, 64, [[260 * ST, 128], [260, ST], [65, 4], [1, 1]]), 1.0), sVA)
        kz = b.op("vector", lambda v: v.memset(k2f[:, :], 0.0), sVA)
        b.wait("scalar", sVA, kz)

        sDXc = (b.sem("sDXc0"), b.sem("sDXc1"))
        sQS2 = b.sem("sQS2")
        qspill2_mark = 0
        tr_ev2 = []
        x1free = {}
        x1ld = []
        for i in range(ST):
            slot = i % 2
            xap = x1b[:, slot * D:(slot + 1) * D]
            if i >= 2:
                b.wait("sync", sVA, x1free[i - 2])
            bsrc = x_in if "BX" in phases else x1h
            x1ld.append(b.dma("sync", xap, bsrc[i * 128:(i + 1) * 128, :], sDXc[slot]))
            b.wait("vector", sDXc[slot], x1ld[i])
            if i >= 1:
                b.wait("vector", sSA)
            b.wait("scalar", sPE)
            ln_t, xc_t = emit_ln(xap, xap, ln2tm[:], st3, m2s[:, i:i + 1],
                                 r2s[:, i:i + 1], ln2tm[:], CEPS5, None)
            x1free[i] = ln_t
            w = i % 4
            b.wait("tensor", sVA, ln_t)
            if i % 4 == 0 and i > 0:
                b.wait("scalar", sQS2, qspill2_mark)
            for c in range(8):
                pt = (ptA2, ptB2)[c % 2]
                if len(tr_ev2) >= 2:
                    b.wait("tensor", sVA, tr_ev2[-2])
                tp = b.op("tensor", lambda t, pt=pt, c=c: t.transpose(
                    pt[:, :], ln2tm[:, c * 128:(c + 1) * 128], c_idb[:]), sPE)
                b.wait("vector", sPE, tp)
                tr_ev2.append(b.op("vector", lambda v, pt=pt, c=c, w=w: v.tensor_copy(
                    lnf2[:, c * 512 + w * 128: c * 512 + w * 128 + 128], pt[:, :]), sVA))
            if w == 3:
                i4 = i // 4
                b.wait("tensor", sVA, tr_ev2[-1])
                ev_q = ev_k = None
                for oc in range(8):
                    if ev_q is not None:
                        b.wait("tensor", sSA, ev_q)
                    for c in range(8):
                        b.op("tensor", lambda t, oc=oc, c=c: t.matmul(
                            pq0[:, :], qw_s[:, c * D + oc * 128: c * D + oc * 128 + 128],
                            lnf2[:, c * 512:(c + 1) * 512], start=(c == 0), stop=False))
                    tq = b.op("tensor", lambda t, oc=oc: t.matmul(
                        pq0[:, :], c_qbr[0:1, oc * 128:(oc + 1) * 128], c_or[:],
                        start=False, stop=True), sPE)
                    b.wait("scalar", sPE, tq)
                    ev_q = b.op("scalar", lambda s, oc=oc, i4=i4: s.activation(
                        qwin2[:, oc * 512:(oc + 1) * 512], pq0[:, :],
                        AF.Copy, scale=CINV8), sSA)
                for oc in range(2):
                    if ev_k is not None:
                        b.wait("tensor", sSA, ev_k)
                    for c in range(8):
                        b.op("tensor", lambda t, oc=oc, c=c: t.matmul(
                            pq1[:, :], kw_s[:, c * 256 + oc * 128: c * 256 + oc * 128 + 128],
                            lnf2[:, c * 512:(c + 1) * 512], start=(c == 0), stop=False))
                    tk = b.op("tensor", lambda t, oc=oc: t.matmul(
                        pq1[:, :], c_kbr[0:1, oc * 128:(oc + 1) * 128], c_or[:],
                        start=False, stop=True), sPE)
                    b.wait("scalar", sPE, tk)
                    b.op("scalar", lambda s, oc=oc, i4=i4: s.activation(
                        k2f[0:64, (2 * oc) * S + i4 * 512: (2 * oc) * S + i4 * 512 + 512],
                        pq1[0:64, :], AF.Copy), sSA)
                    ev_k = b.op("scalar", lambda s, oc=oc, i4=i4: s.activation(
                        k2f[64:128, (2 * oc + 1) * S + i4 * 512: (2 * oc + 1) * S + i4 * 512 + 512],
                        pq1[64:128, :], AF.Copy), sSA)
                b.wait("sync", sSA, ev_q)
                for oc in range(8):
                    b.dma("sync", qsp[oc, :, i4 * 512:(i4 + 1) * 512],
                          qwin2[:, oc * 512:(oc + 1) * 512], sQS2)
                qspill2_mark = sQS2.v
                ev_v = None
                for sc in range(4):
                    st_idx = i4 * 4 + sc
                    if ev_v is not None:
                        b.wait("tensor", sVA, ev_v)
                    for c in range(8):
                        b.op("tensor", lambda t, sc=sc, c=c: t.matmul(
                            pq2[:, :], lnf2[:, c * 512 + sc * 128: c * 512 + sc * 128 + 128],
                            vw_s[:, c * 256:(c + 1) * 256], start=(c == 0), stop=False))
                    tv = b.op("tensor", lambda t, sc=sc: t.matmul(
                        pq2[:, :], c_or[0:1, 0:128], c_vbr[:], start=False, stop=True), sPE)
                    b.wait("vector", sPE, tv)
                    b.wait("vector", sVA, vones)
                    for g in range(4):
                        ev_v = b.op("vector", lambda v, st_idx=st_idx, g=g: v.tensor_copy(
                            v2t[:, st_idx * 260 + g * 65: st_idx * 260 + g * 65 + 64],
                            pq2[:, g * 64:(g + 1) * 64]), sVA)
        QKV_SA = sSA.v
        QKV_VA = sVA.v
        pB1.close()

        pB4 = ExitStack()
        qblk = b.sb("qblk", [128, 2 * 512], BF16, pB4)
        ptb0 = b.sb("ptb0", [128, 1536], BF16, pB4)
        ptb1 = b.sb("ptb1", [128, 1536], BF16, pB4)
        pvsb = b.sb("pvsb", [128, 2 * 512], F32, pB4)
        rlb = b.sb("rlb", [128, 1], F32, pB4)
        psA = b.ps("psA", [128, 1536], F32, pB4)
        psB = b.ps("psB", [128, 1024], F32, pB4)
        ppv2a = b.ps("ppv2a", [128, 512], F32, pB4)
        ppv2b = b.ps("ppv2b", [128, 512], F32, pB4)
        pvt = b.ps("pvt", [128, 260], F32, pB4)
        b.wait("tensor", sSA, QKV_SA)
        b.wait("tensor", sVA, QKV_VA)
        b.wait("sync", sQS2, sQS2.v)
        sQL = (b.sem("sQL0"), b.sem("sQL1"))
        qrel = {}
        qld_t = {}
        it = 0
        ITERS = [(h, qb) for h in range(NH) for qb in range(S // 512)]
        _h0, _qb0 = ITERS[0]
        qld_t[0] = b.dma("sync", qblk[:, 0:512],
                         qsp[HEAD_SLOT[_h0][0], :, _qb0 * 512:(_qb0 + 1) * 512], sQL[0])
        bankB = {0: 0, 1: 0}
        ptbB = {0: 0, 1: 0}
        pv_copy = {0: 0, 1: 0}
        pvt_free = 0
        pend_tail = [None]
        NSK = S // 128
        for h in range(NH):
            jslot, p = HEAD_SLOT[h]
            g = h // 4
            for qb in range(S // 512):
                qsl = it % 2
                if it + 1 < len(ITERS):
                    nh, nqb = ITERS[it + 1]
                    nsl = (it + 1) % 2
                    if it >= 1:
                        b.wait("sync", sPE, qrel[it - 1])
                    qld_t[it + 1] = b.dma("sync", qblk[:, nsl * 512:(nsl + 1) * 512],
                                          qsp[HEAD_SLOT[nh][0], :, nqb * 512:(nqb + 1) * 512],
                                          sQL[nsl])
                b.wait("tensor", sQL[qsl], qld_t[it])
                groups = []
                rem = NSK
                while rem > 0:
                    gsz = 3 if len(groups) % 2 == 0 else 2
                    gsz = min(gsz, rem)
                    groups.append(gsz)
                    rem -= gsz
                starts = []
                acc = 0
                for gw in groups:
                    starts.append(acc)
                    acc += gw
                exp_marks = {}

                def emit_scores(g2, p=p, jslot=jslot, qb=qb, qsl=qsl, g=g):
                    gw = groups[g2]
                    bk = g2 % 2
                    psc = (psA, psB)[bk]
                    b.wait("tensor", sSA, bankB[bk])
                    if g2 == 0:
                        b.wait("tensor", sVA, pv_copy[it % 2])
                    for cc in range(gw):
                        skt = starts[g2] + cc
                        tmm = b.op("tensor", lambda t, cc=cc, skt=skt, psc=psc, g=g,
                                   p=p, jslot=jslot, qb=qb, qsl=qsl: t.matmul(
                            psc[:, cc * 512:(cc + 1) * 512],
                            k2f[:, g * S + skt * 128: g * S + skt * 128 + 128],
                            qblk[:, qsl * 512:(qsl + 1) * 512],
                            start=True, stop=True), sPE if cc == gw - 1 else None)
                    ptb = (ptb0, ptb1)[bk]
                    b.wait("scalar", sPE, tmm)
                    b.wait("scalar", sPE, ptbB[bk])
                    gw_ = gw
                    eexp = b.op("scalar", lambda s, psc=psc, ptb=ptb, gw_=gw_: s.activation(
                        ptb[:, 0:gw_ * 512], psc[:, 0:gw_ * 512], AF.Exp, bias=CNEGC), sSA)
                    bankB[bk] = eexp
                    exp_marks[g2] = eexp

                ppv_it = (ppv2a, ppv2b)[it % 2]

                def emit_pv4(g2, g=g, ppv_it=ppv_it):
                    gw = groups[g2]
                    bk = g2 % 2
                    ptb = (ptb0, ptb1)[bk]
                    b.wait("tensor", sSA, exp_marks[g2])
                    for cc in range(gw):
                        skt = starts[g2] + cc
                        pvm = b.op("tensor", lambda t, cc=cc, skt=skt, ptb=ptb, g=g,
                                   ppv_it=ppv_it: t.matmul(
                            ppv_it[0:65, :], v2t[:, skt * 260 + g * 65: skt * 260 + g * 65 + 65],
                            ptb[:, cc * 512:(cc + 1) * 512],
                            start=(skt == 0), stop=(skt == NSK - 1)),
                            sPE if cc == gw - 1 else None)
                    ptbB[bk] = pvm
                    return pvm

                for g2 in range(len(groups)):
                    emit_scores(g2)
                    if g2 >= 1:
                        pvm = emit_pv4(g2 - 1)
                pvm = emit_pv4(len(groups) - 1)
                qrel[it] = sPE.v
                it += 1

                def emit_tail(ent):
                    tit, th, tqb, tpvm = ent
                    tjslot, tp = HEAD_SLOT[th]
                    ppv = (ppv2a, ppv2b)[tit % 2]
                    psl = tit % 2
                    b.wait("vector", sPE, tpvm)
                    pc = b.op("vector", lambda v, ppv=ppv, psl=psl: v.tensor_copy(
                        pvsb[0:65, psl * 512:(psl + 1) * 512], ppv[0:65, :]), sVA)
                    pv_copy[psl] = pc
                    b.wait("tensor", sVA, pc)
                    b.wait("tensor", sVA, pvt_free)
                    for cc in range(4):
                        tvt = b.op("tensor", lambda t, cc=cc, psl=psl: t.transpose(
                            pvt[:, cc * 65:(cc + 1) * 65],
                            pvsb[0:65, psl * 512 + cc * 128: psl * 512 + (cc + 1) * 128],
                            c_idf[0:65, 0:65]), sPE)
                    b.wait("vector", sPE, tvt)
                    out_free_t = 0
                    for cc in range(4):
                        st_idx = tqb * 4 + cc
                        b.wait("vector", sVA)
                        b.op("vector", lambda v, cc=cc: v.reciprocal(
                            rlb[:], pvt[:, cc * 65 + 64: cc * 65 + 65]), sVA)
                        b.wait("vector", sVA)
                        out_free_t = b.op("vector", lambda v, cc=cc, st_idx=st_idx,
                                          tjslot=tjslot, tp=tp: v.tensor_scalar_mul(
                            attc[:, st_idx * D + tjslot * 128 + tp: st_idx * D + tjslot * 128 + tp + 64],
                            pvt[:, cc * 65: cc * 65 + 64], rlb[:]), sVA)
                    return out_free_t

                if pend_tail[0] is not None:
                    pvt_free = emit_tail(pend_tail[0])
                pend_tail[0] = (it - 1, h, qb, pvm)
        if pend_tail[0] is not None:
            pvt_free = emit_tail(pend_tail[0])
        B4_VA = sVA.v
        if "BX" in phases:
            devb = ExitStack()
            sDVB = b.sem("sDVB")
            b.wait("gpsimd", sVA, B4_VA)
            for i in range(ST):
                st_ = b.dma("gpsimd", out_ap[i * 128:(i + 1) * 128, :],
                            attc[:, i * D:(i + 1) * D], sDVB)
            b.wait("gpsimd", sDVB, sDVB.v)
            devb.close()
            pB4.close()
            pQKV.close()
            pATT.close()
            phB.close()
            with nc.Block() as block:
                @block.sync
                def _(e):
                    for f in b.ops["sync"]:
                        f(e)

                @block.gpsimd
                def _(e):
                    for f in b.ops["gpsimd"]:
                        f(e)

                @block.vector
                def _(e):
                    for f in b.ops["vector"]:
                        f(e)

                @block.scalar
                def _(e):
                    for f in b.ops["scalar"]:
                        f(e)

                @block.tensor
                def _(e):
                    for f in b.ops["tensor"]:
                        f(e)
            ctx.close()
            return nc
        pB4.close()
        pQKV.close()

        pSRC2 = ExitStack()
        src2bf = b.sb("src2bf", [128, ST * D], BF16, pSRC2)
        pB5 = ExitStack()
        ow_s = b.sb("ow_s", [128, 8 * D], BF16, pB5)
        attf = b.sb("attf", [128, 8 * 128], BF16, pB5)
        x1b2 = b.sb("x1b2", [128, 2 * D], F32, pB5)
        ln2t = b.sb("ln2t", [128, D], F32, pB5)
        src1f = b.sb("src1f", [128, D], F32, pB5)
        src2f = b.sb("src2f", [128, 2 * D], F32, pB5)
        s2T = b.sb("s2T", [128, 8 * 128], F32, pB5)
        sqb5 = b.sb("sqb5", [128, D], BF16, pB5)
        st5 = b.sb("st5", [128, 4], F32, pB5)
        pta5 = b.ps("pta5", [128, 128], BF16, pB5)
        ptb5 = b.ps("ptb5", [128, 128], BF16, pB5)
        pgq0 = b.ps("pgq0", [128, 512], F32, pB5)
        pgq1 = b.ps("pgq1", [128, 512], F32, pB5)
        ptf0 = b.ps("ptf0", [128, 128], F32, pB5)
        ptf1 = b.ps("ptf1", [128, 128], F32, pB5)
        pgl = b.ps("pgl", [128, 8], F32, pB5)
        sLB5 = b.sem("sLB5")
        for _e in ("gpsimd", "sync"):
            b.wait(_e, sVA, B4_VA)
            b.wait(_e, sPE, sPE.v)
            b.wait(_e, sSA, sSA.v)
        for c in range(8):
            b.dma("gpsimd", ow_s[:, c * D:(c + 1) * D], ow[c * 128:(c + 1) * 128, :], sLB5)
        for e in ("tensor",):
            b.wait(e, sLB5, sLB5.v)
        b.wait("tensor", sVA, B4_VA)
        sDXd = (b.sem("sDXd0"), b.sem("sDXd1"))
        sS2 = (b.sem("sS20"), b.sem("sS21"))
        tr_ev5 = []
        s2_free = {}
        tf_mark = {0: 0, 1: 0}
        for i in range(ST):
            slot = i % 2
            if i >= 2:
                b.wait("sync", sVA, s2_free[i - 2])
            xld5 = b.dma("sync", x1b2[:, slot * D:(slot + 1) * D],
                         x1h[i * 128:(i + 1) * 128, :], sDXd[slot])
            for c in range(8):
                pt = (pta5, ptb5)[c % 2]
                if len(tr_ev5) >= 2:
                    b.wait("tensor", sVA, tr_ev5[-2])
                tp = b.op("tensor", lambda t, i=i, c=c, pt=pt: t.transpose(
                    pt[:, :], attc[:, i * D + c * 128: i * D + c * 128 + 128], c_idb[:]), sPE)
                b.wait("vector", sPE, tp)
                tr_ev5.append(b.op("vector", lambda v, c=c, pt=pt: v.tensor_copy(
                    attf[:, c * 128:(c + 1) * 128], pt[:, :]), sVA))
            b.wait("tensor", sVA, tr_ev5[-1])
            for half, pgq in ((0, pgq0), (1, pgq1)):
                for c in range(8):
                    b.op("tensor", lambda t, half=half, c=c, pgq=pgq: t.matmul(
                        pgq[:, :], attf[:, c * 128:(c + 1) * 128],
                        ow_s[:, c * D + half * 512: c * D + half * 512 + 512],
                        start=(c == 0), stop=False))
                tow = b.op("tensor", lambda t, half=half, pgq=pgq: t.matmul(
                    pgq[:, :], c_or[0:1, 0:128], c_obr[0:1, half * 512: half * 512 + 512],
                    start=False, stop=True), sPE)
            b.wait("vector", sDXd[slot], xld5)
            b.op("vector", lambda v, i=i, slot=slot: v.tensor_scalar(
                ln2t[:], x1b2[:, slot * D:(slot + 1) * D], m2s[:, i:i + 1], None,
                op0=ALU.subtract), sVA)
            b.wait("vector", sVA)
            b.op("vector", lambda v, i=i: v.tensor_scalar_mul(ln2t[:], ln2t[:], r2s[:, i:i + 1]), sVA)
            b.wait("vector", sVA)
            b.wait("vector", sPE, tow)
            for half, pgq in ((0, pgq0), (1, pgq1)):
                b.op("vector", lambda v, half=half, pgq=pgq: v.tensor_add(
                    src1f[:, half * 512:(half + 1) * 512], pgq[:, :],
                    ln2t[:, half * 512:(half + 1) * 512]), sVA)
            src1_t = sVA.v
            b.wait("scalar", sVA, src1_t)
            b.op("scalar", lambda s: s.activation(sqb5[:], src1f[:], AF.Square,
                                                  accum_out=st5[:, 0:1]), sSA)
            b.wait("scalar", sSA)
            b.op("scalar", lambda s: s.activation(st5[:, 1:2], st5[:, 0:1], AF.Ln,
                                                  scale=CINVD, bias=CEPS6), sSA)
            b.wait("scalar", sSA)
            b.op("scalar", lambda s: s.activation(st5[:, 2:3], st5[:, 1:2], AF.Exp,
                                                  scale=CNEGH), sSA)
            r3_t = sSA.v
            b.wait("vector", sSA, r3_t)
            b.wait("vector", sPE, tf_mark[slot])
            b.wait("vector", sS2[slot], sS2[slot].v)
            s2t = b.op("vector", lambda v, slot=slot: v.scalar_tensor_tensor(
                src2f[:, slot * D:(slot + 1) * D], src1f[:], st5[:, 2:3], c_n1[:],
                op0=ALU.mult, op1=ALU.mult), sVA)
            b.wait("vector", sVA, s2t)
            s2_free[i] = b.op("vector", lambda v, i=i, slot=slot: v.tensor_copy(
                src2bf[:, i * D:(i + 1) * D], src2f[:, slot * D:(slot + 1) * D]), sVA)
            b.wait("sync", sVA, s2_free[i])
            b.dma("sync", s2h[i * 128:(i + 1) * 128, :], src2bf[:, i * D:(i + 1) * D], sS2[slot])
            b.wait("tensor", sVA, s2_free[i])
            for c in range(8):
                ptf = (ptf0, ptf1)[c % 2]
                if c >= 2:
                    b.wait("tensor", sSA)
                tpf = b.op("tensor", lambda t, c=c, slot=slot, ptf=ptf: t.transpose(
                    ptf[:, :], src2f[:, slot * D + c * 128: slot * D + c * 128 + 128],
                    c_idf[:]), sPE)
                b.wait("scalar", sPE, tpf)
                b.op("scalar", lambda s, c=c, ptf=ptf: s.activation(
                    s2T[:, c * 128:(c + 1) * 128], ptf[:, :], AF.Copy), sSA)
            tf_mark[slot] = sPE.v
            b.wait("tensor", sSA)
            for c in range(8):
                tgl = b.op("tensor", lambda t, c=c: t.matmul(
                    pgl[:, :], s2T[:, c * 128:(c + 1) * 128], c_gw[:, c * 8:(c + 1) * 8],
                    start=(c == 0), stop=(c == 7)), sPE if c == 7 else None)
            b.wait("vector", sPE, tgl)
            gl_ev = b.op("vector", lambda v, i=i: v.tensor_copy(
                logits[:, i * 8:(i + 1) * 8], pgl[:, :]), sVA)
            b.wait("tensor", sVA, gl_ev)
        B5_VA = sVA.v
        B5_SS = (sS2[0], sS2[0].v, sS2[1], sS2[1].v)
        pB5.close()

        if "src2" in tap_map:
            b.wait("sync", sS2[0], B5_SS[1])
            b.wait("sync", sS2[1], B5_SS[3])
            tb2 = b.sb("tapb2", [128, D], BF16)
            sTPg = b.sem("sTPg")
            for i in range(ST):
                t = b.dma("gpsimd", tb2[:], s2h[i * 128:(i + 1) * 128, :], sTPg)
                b.wait("gpsimd", sTPg, t)
                t2 = b.dma("gpsimd", tap_map["src2"][i * 128:(i + 1) * 128, :], tb2[:], sTPg)
                b.wait("gpsimd", sTPg, t2)

        pRT = ExitStack()
        E8 = ST * 8
        m8s = b.sb("m8s", [128, 8], F32, pRT)
        i8s = b.sb("i8s", [128, 8], U32, pRT)
        idx0f = b.sb("idx0f", [128, ST], F32, pRT)
        idx1f = b.sb("idx1f", [128, ST], F32, pRT)
        dcol = b.sb("dcol", [128, 1], F32, pRT)
        ind0 = b.sb("ind0", [128, E8], F32, pRT)
        ind1 = b.sb("ind1", [128, E8], F32, pRT)
        indsum = b.sb("indsum", [128, E8], F32, pRT)
        idsb = b.sb("idsb", [128, E8], BF16, pRT)
        Ssb = b.sb("Ssb", [128, E8], F32, pRT)
        Tsb = b.sb("Tsb", [1, E8], F32, pRT)
        crow0 = b.sb("crow0", [1, E8], F32, pRT)
        crow1 = b.sb("crow1", [1, E8], F32, pRT)
        cntrow = b.sb("cntrow", [1, 8], F32, pRT)
        aggsb = b.sb("aggsb", [NCORES, 8], F32, pRT)
        offrow = b.sb("offrow", [1, 8], F32, pRT)
        bcrow = b.sb("bcrow", [1, 2 * E8 if 2 * E8 >= 2 * NBLK else 2 * NBLK], F32, pRT)
        zrow = b.sb("zrow", [1, ST], F32, pRT)
        cbb = b.sb("cbb", [128, 2 * E8 if 2 * E8 >= 2 * NBLK else 2 * NBLK], F32, pRT)
        gincl = b.sb("gincl", [128, E8], F32, pRT)
        rank0 = b.sb("rank0", [128, E8], F32, pRT)
        rank1 = b.sb("rank1", [128, E8], F32, pRT)
        keep0 = b.sb("keep0", [128, E8], F32, pRT)
        keep1 = b.sb("keep1", [128, E8], F32, pRT)
        ksum = b.sb("ksum", [128, E8], BF16, pRT)
        vcrow = b.sb("vcrow", [1, 8], F32, pRT)
        nbtmp = b.sb("nbtmp", [1, 16], F32, pRT)
        nbrow = b.sb("nbrow", [1, 8], F32, pRT)
        bsrow = b.sb("bsrow", [1, 8], F32, pRT)
        berow = b.sb("berow", [1, 8], F32, pRT)
        barow = b.sb("barow", [1, 8], F32, pRT)
        eblk = b.sb("eblk", [1, NBLK], F32, pRT)
        ebt = b.sb("ebt", [1, NBLK], F32, pRT)
        sel0 = b.sb("sel0", [128, ST], F32, pRT)
        sel1 = b.sb("sel1", [128, ST], F32, pRT)
        ksel0 = b.sb("ksel0", [128, ST], F32, pRT)
        ksel1 = b.sb("ksel1", [128, ST], F32, pRT)
        poss0 = b.sb("poss0", [128, ST], F32, pRT)
        poss1 = b.sb("poss1", [128, ST], F32, pRT)
        psS = b.ps("psS", [128, E8], F32, pRT)
        psT = b.ps("psT", [1, E8], F32, pRT)
        psO = b.ps("psO", [1, 8], F32, pRT)
        psBC = b.ps("psBC", [128, 2 * E8 if 2 * E8 >= 2 * NBLK else 2 * NBLK], F32, pRT)

        b.wait("scalar", sVA, B5_VA)
        b.wait("vector", sLB5, sLB5.v)
        b.wait("scalar", sLB5, sLB5.v)
        b.op("scalar", lambda s: s.activation(probs[:], logits[:], AF.Sigmoid), sSA)
        PRB = sSA.v
        if "probs" in tap_map:
            b.wait("sync", sSA, PRB)
            b.dma("sync", tap_map["probs"], probs[:], b.sem("sTPP"))
        b.wait("vector", sSA, PRB)
        for f in range(ST):
            b.op("vector", lambda v, f=f: v.max(out=m8s[:], in_=probs[:, f * 8:(f + 1) * 8]), sVA)
            b.wait("vector", sVA)
            b.op("vector", lambda v, f=f: v.max_index(out=i8s[:], in_max=m8s[:],
                                                 in_values=probs[:, f * 8:(f + 1) * 8]), sVA)
            b.wait("vector", sVA)
            b.op("vector", lambda v, f=f: v.tensor_copy(idx0f[:, f:f + 1], i8s[:, 0:1].bitcast(I32)), sVA)
            b.op("vector", lambda v, f=f: v.tensor_copy(idx1f[:, f:f + 1], i8s[:, 1:2].bitcast(I32)), sVA)
            b.op("vector", lambda v: v.tensor_add(dcol[:], m8s[:, 0:1], m8s[:, 1:2]), sVA)
            b.wait("vector", sVA)
            b.op("vector", lambda v: v.tensor_scalar_add(dcol[:], dcol[:], 1e-6), sVA)
            b.wait("vector", sVA)
            b.op("vector", lambda v: v.reciprocal(dcol[:], dcol[:]), sVA)
            b.wait("vector", sVA)
            b.op("vector", lambda v, f=f: v.tensor_scalar_mul(w0c[:, f:f + 1], m8s[:, 0:1], dcol[:]), sVA)
            b.op("vector", lambda v, f=f: v.tensor_scalar_mul(w1c[:, f:f + 1], m8s[:, 1:2], dcol[:]), sVA)
            b.wait("vector", sVA)
        for e in range(NEXP):
            b.op("vector", lambda v, e=e: v.tensor_scalar(
                bass.AP(ind0.ap().tensor, e, [[E8, 128], [8, ST]]), idx0f[:],
                float(e), None, op0=ALU.is_equal), sVA)
            b.op("vector", lambda v, e=e: v.tensor_scalar(
                bass.AP(ind1.ap().tensor, e, [[E8, 128], [8, ST]]), idx1f[:],
                float(e), None, op0=ALU.is_equal), sVA)
        b.wait("vector", sVA)
        b.op("vector", lambda v: v.tensor_add(indsum[:], ind0[:], ind1[:]), sVA)
        b.wait("vector", sVA)
        ids_t = b.op("vector", lambda v: v.tensor_copy(idsb[:], indsum[:]), sVA)
        b.wait("tensor", sVA, ids_t)
        tS = b.op("tensor", lambda t: t.matmul(psS[:], c_triu[:], idsb[:], start=True, stop=True), sPE)
        tT = b.op("tensor", lambda t: t.matmul(psT[:], c_oc[:], idsb[:], start=True, stop=True), sPE)
        b.wait("vector", sPE, tT)
        b.op("vector", lambda v: v.tensor_copy(Ssb[:], psS[:]), sVA)
        b.op("vector", lambda v: v.tensor_copy(Tsb[:], psT[:]), sVA)
        b.wait("vector", sVA)
        b.op("vector", lambda v: v.memset(crow0[:], 0.0), sVA)
        b.wait("vector", sVA)
        b.op("vector", lambda v: v.tensor_copy(crow0[0:1, 8:E8], Tsb[0:1, 0:E8 - 8]), sVA)
        b.wait("vector", sVA)
        cur, nxt = crow0, crow1
        sh = 8
        while sh < E8:
            b.op("vector", lambda v, cur=cur, nxt=nxt, sh=sh: v.tensor_copy(
                nxt[0:1, 0:sh], cur[0:1, 0:sh]), sVA)
            b.op("vector", lambda v, cur=cur, nxt=nxt, sh=sh: v.tensor_add(
                nxt[0:1, sh:E8], cur[0:1, sh:E8], cur[0:1, 0:E8 - sh]), sVA)
            b.wait("vector", sVA)
            cur, nxt = nxt, cur
            sh *= 2
        carry = cur
        for e in range(NEXP):
            b.op("vector", lambda v, e=e: v.reduce_sum(
                cntrow[0:1, e:e + 1], bass.AP(Tsb.ap().tensor, e, [[E8, 1], [8, ST]]),
                axis=AX.X), sVA)
        CNT_VA = sVA.v
        sAGD = b.sem("sAGD")
        b.wait("gpsimd", sVA, CNT_VA)
        b.dma("gpsimd", ag_in, cntrow[:], sAGD)
        b.wait("gpsimd", sAGD, sAGD.v)
        b.op("gpsimd", lambda g: g.collective_compute(
            "AllGather", ALU.bypass, replica_groups=[list(range(NCORES))],
            ins=[ag_in], outs=[ag_out]), sCC)
        b.wait("gpsimd", sCC, sCC.v)
        agld = b.dma("gpsimd", aggsb[:], ag_out, sAGD)
        b.wait("tensor", sAGD, agld)
        tO = b.op("tensor", lambda t: t.matmul(psO[:], c_cm[:], aggsb[:], start=True, stop=True), sPE)
        b.wait("vector", sPE, tO)
        b.op("vector", lambda v: v.tensor_copy(offrow[:], psO[:]), sVA)
        b.wait("vector", sVA)
        for e in range(NEXP):
            b.op("vector", lambda v, e=e, carry=carry: v.tensor_scalar_add(
                bass.AP(bcrow.ap().tensor, e, [[bcrow.shape[1], 1], [8, ST]]),
                bass.AP(carry.ap().tensor, e, [[256, 1], [8, ST]]),
                offrow[0:1, e:e + 1]), sVA)
        b.op("vector", lambda v: v.memset(zrow[:], 0.0), sVA)
        BC1 = sVA.v
        b.wait("vector", sVA, BC1)
        bc_t = sVA.v
        b.wait("tensor", sVA, BC1)
        tBC = b.op("tensor", lambda t: t.matmul(psBC[:, 0:E8], c_orf[:], bcrow[0:1, 0:E8], start=True, stop=True), sPE)
        b.wait("vector", sPE, tBC)
        b.op("vector", lambda v: v.tensor_copy(cbb[:, 0:E8], psBC[:, 0:E8]), sVA)
        b.wait("vector", sVA)
        b.op("vector", lambda v: v.tensor_add(gincl[:], Ssb[:], cbb[:, 0:E8]), sVA)
        b.wait("vector", sVA)
        b.op("vector", lambda v: v.tensor_sub(rank0[:], gincl[:], indsum[:]), sVA)
        b.op("vector", lambda v: v.tensor_sub(rank1[:], gincl[:], ind1[:]), sVA)
        b.wait("vector", sVA)
        b.op("vector", lambda v: v.tensor_scalar(keep0[:], rank0[:], float(CAP - 1), None,
                                                 op0=ALU.is_le), sVA)
        b.op("vector", lambda v: v.tensor_scalar(keep1[:], rank1[:], float(CAP - 1), None,
                                                 op0=ALU.is_le), sVA)
        b.wait("vector", sVA)
        b.op("vector", lambda v: v.tensor_mul(keep0[:], keep0[:], ind0[:]), sVA)
        b.op("vector", lambda v: v.tensor_mul(keep1[:], keep1[:], ind1[:]), sVA)
        b.wait("vector", sVA)
        b.op("vector", lambda v: v.tensor_add(ksum[:], keep0[:], keep1[:]), sVA)
        KS = sVA.v
        b.wait("tensor", sVA, KS)
        tT2 = b.op("tensor", lambda t: t.matmul(psT[:], c_oc[:], ksum[:], start=True, stop=True), sPE)
        b.wait("vector", sPE, tT2)
        b.op("vector", lambda v: v.tensor_copy(Tsb[:], psT[:]), sVA)
        b.wait("vector", sVA)
        for e in range(NEXP):
            b.op("vector", lambda v, e=e: v.reduce_sum(
                vcrow[0:1, e:e + 1], bass.AP(Tsb.ap().tensor, e, [[E8, 1], [8, ST]]),
                axis=AX.X), sVA)
        b.wait("vector", sVA)
        for e in range(NEXP):
            b.op("vector", lambda v, e=e: v.tensor_scalar(
                nbtmp[:], c_thr[:], vcrow[0:1, e:e + 1], None, op0=ALU.is_lt), sVA)
            b.wait("vector", sVA)
            b.op("vector", lambda v, e=e: v.reduce_sum(nbrow[0:1, e:e + 1], nbtmp[:], axis=AX.X), sVA)
            b.wait("vector", sVA)
        b.op("vector", lambda v, bs=bsrow: v.tensor_copy(bs[:], nbrow[:]), sVA)
        b.wait("vector", sVA)
        for sh in (1, 2, 4):
            b.op("vector", lambda v, sh=sh, bs=bsrow, be=berow: v.tensor_copy(
                be[0:1, 0:sh], bs[0:1, 0:sh]), sVA)
            b.op("vector", lambda v, sh=sh, bs=bsrow, be=berow: v.tensor_add(
                be[0:1, sh:8], bs[0:1, sh:8], bs[0:1, 0:8 - sh]), sVA)
            b.wait("vector", sVA)
            bsrow, berow = berow, bsrow
        b.op("vector", lambda v, bs=bsrow: v.tensor_sub(bs[:], bs[:], nbrow[:]), sVA)
        b.wait("vector", sVA)
        b.op("vector", lambda v, bs=bsrow, be=berow: v.tensor_add(be[:], bs[:], nbrow[:]), sVA)
        b.wait("vector", sVA)
        b.op("vector", lambda v, bs=bsrow: v.tensor_scalar_mul(barow[:], bs[:], float(CBLK)), sVA)
        b.wait("vector", sVA)
        b.op("vector", lambda v: v.tensor_sub(barow[:], barow[:], offrow[:]), sVA)
        b.wait("vector", sVA)
        for e in range(NEXP):
            b.op("vector", lambda v, e=e: v.tensor_scalar_add(
                bass.AP(bcrow.ap().tensor, E8 + e, [[bcrow.shape[1], 1], [8, ST]]),
                zrow[0:1, :], barow[0:1, e:e + 1]), sVA)
        BC2 = sVA.v
        b.wait("tensor", sVA, BC2)
        tBC2 = b.op("tensor", lambda t: t.matmul(psBC[:, E8:2 * E8], c_orf[:], bcrow[0:1, E8:2 * E8], start=True, stop=True), sPE)
        b.wait("vector", sPE, tBC2)
        b.op("vector", lambda v: v.tensor_copy(cbb[:, E8:2 * E8], psBC[:, E8:2 * E8]), sVA)
        b.wait("vector", sVA)
        b.op("vector", lambda v: v.memset(eblk[:], 0.0), sVA)
        b.wait("vector", sVA)
        for e in range(1, NEXP):
            b.op("vector", lambda v, e=e, bs=bsrow: v.tensor_scalar(
                ebt[:], c_js[:], bs[0:1, e:e + 1], None, op0=ALU.is_ge), sVA)
            b.op("vector", lambda v, e=e, be=berow: v.tensor_scalar(
                wofff[:], c_js[:], be[0:1, e:e + 1], None, op0=ALU.is_lt), sVA)
            b.wait("vector", sVA)
            b.op("vector", lambda v: v.tensor_mul(ebt[:], ebt[:], wofff[:]), sVA)
            b.wait("vector", sVA)
            b.op("vector", lambda v, e=e: v.tensor_scalar(
                ebt[:], ebt[:], float(e), None, op0=ALU.mult), sVA)
            b.wait("vector", sVA)
            b.op("vector", lambda v: v.tensor_add(eblk[:], eblk[:], ebt[:]), sVA)
            b.wait("vector", sVA)
        b.op("vector", lambda v: v.tensor_scalar(
            wobase[0:1, 0:NBLK], eblk[:], 128.0, None, op0=ALU.mult), sVA)
        WOB = sVA.v
        b.wait("tensor", sVA, WOB)
        tWB = b.op("tensor", lambda t: t.matmul(psBC[:, 0:NBLK], c_orf[:],
                                                wobase[0:1, 0:NBLK], start=True, stop=True), sPE)
        b.wait("vector", sPE, tWB)
        b.op("vector", lambda v: v.tensor_scalar(
            widxf[:, 0:NBLK], psBC[:, 0:NBLK], c_iota[:], None, op0=ALU.add), sVA)
        b.wait("vector", sVA)
        woff_t = b.op("vector", lambda v: v.tensor_copy(widxu[:, 0:NBLK], widxf[:, 0:NBLK]), sVA)
        b.op("vector", lambda v: v.tensor_add(rank0[:], rank0[:], cbb[:, E8:2 * E8]), sVA)
        b.op("vector", lambda v: v.tensor_add(rank1[:], rank1[:], cbb[:, E8:2 * E8]), sVA)
        b.wait("vector", sVA)
        b.op("vector", lambda v: v.tensor_mul(rank0[:], rank0[:], ind0[:]), sVA)
        b.op("vector", lambda v: v.tensor_mul(rank1[:], rank1[:], ind1[:]), sVA)
        b.wait("vector", sVA)
        b.op("vector", lambda v: v.tensor_reduce(
            sel0[:], rank0[:].rearrange("p (f e) -> p f e", e=8), axis=AX.X, op=ALU.add), sVA)
        b.op("vector", lambda v: v.tensor_reduce(
            sel1[:], rank1[:].rearrange("p (f e) -> p f e", e=8), axis=AX.X, op=ALU.add), sVA)
        b.op("vector", lambda v: v.tensor_reduce(
            ksel0[:], keep0[:].rearrange("p (f e) -> p f e", e=8), axis=AX.X, op=ALU.add), sVA)
        b.op("vector", lambda v: v.tensor_reduce(
            ksel1[:], keep1[:].rearrange("p (f e) -> p f e", e=8), axis=AX.X, op=ALU.add), sVA)
        b.wait("vector", sVA)
        for sel, ksel, poss, scu in ((sel0, ksel0, poss0, sc0u), (sel1, ksel1, poss1, sc1u)):
            b.op("vector", lambda v, sel=sel, ksel=ksel, poss=poss: v.tensor_mul(
                poss[:], sel[:], ksel[:]), sVA)
            b.op("vector", lambda v, ksel=ksel: v.tensor_scalar(
                idx0f[:], ksel[:], -float(NCROWS), float(NCROWS), op0=ALU.mult, op1=ALU.add), sVA)
            b.wait("vector", sVA)
            b.op("vector", lambda v, poss=poss: v.tensor_add(poss[:], poss[:], idx0f[:]), sVA)
            b.wait("vector", sVA)
            b.op("vector", lambda v, poss=poss, scu=scu: v.tensor_copy(scu[:], poss[:]), sVA)
            b.wait("vector", sVA)
        tFR = b.op("tensor", lambda t: t.matmul(psT[0:128, 0:ST] if False else psS[:, 0:ST],
                                                c_orf[:], c_framp[:], start=True, stop=True), sPE)
        b.wait("vector", sPE, tFR)
        b.op("vector", lambda v: v.tensor_copy(fbb[:], psS[:, 0:ST]), sVA)
        b.wait("vector", sVA)
        b.op("vector", lambda v: v.tensor_scalar(fbb[:], fbb[:], c_iota[:], float(NCROWS),
                                                 op0=ALU.add, op1=ALU.add), sVA)
        b.wait("vector", sVA)
        for sel, ksel, scu, extra in ((sel0, ksel0, ssc0, 0.0), (sel1, ksel1, ssc1, 128.0)):
            b.op("vector", lambda v, sel=sel, extra=extra: v.tensor_scalar(
                idx1f[:], fbb[:], extra, None, op0=ALU.add), sVA)
            b.wait("vector", sVA)
            b.op("vector", lambda v, sel=sel: v.tensor_sub(idx0f[:], sel[:], idx1f[:]), sVA)
            b.wait("vector", sVA)
            b.op("vector", lambda v, ksel=ksel: v.tensor_mul(idx0f[:], idx0f[:], ksel[:]), sVA)
            b.wait("vector", sVA)
            b.op("vector", lambda v: v.tensor_add(idx0f[:], idx0f[:], idx1f[:]), sVA)
            b.wait("vector", sVA)
            b.op("vector", lambda v, scu=scu: v.tensor_copy(scu[:], idx0f[:]), sVA)
            b.wait("vector", sVA)
        ROUTE_VA = sVA.v
        if "route" in tap_map:
            rt = b.sb("rtap", [128, 256], F32, pRT)
            for ci, src_ in enumerate((idx1f, sel0, sel1, ksel0, ksel1, poss0, poss1, w0c)):
                b.op("vector", lambda v, ci=ci, src_=src_: v.tensor_copy(
                    rt[:, ci * 32:(ci + 1) * 32], src_[:]), sVA)
            b.wait("sync", sVA)
            b.dma("sync", tap_map["route"], rt[:], b.sem("sTPR"))
        if "cnt" in tap_map:
            ct = b.sb("ctap", [1, 24], F32, pRT)
            b.op("vector", lambda v: v.tensor_copy(ct[0:1, 0:8], vcrow[:]), sVA)
            b.op("vector", lambda v: v.tensor_copy(ct[0:1, 8:16], nbrow[:]), sVA)
            b.op("vector", lambda v: v.tensor_copy(ct[0:1, 16:24], offrow[:]), sVA)
            b.wait("sync", sVA)
            b.dma("sync", tap_map["cnt"], ct[:], b.sem("sTPC"))
        pRT.close()
        if "C" not in phases:
            pSRC2.close()
            pATT.close()

    if "C" in phases:
        sSCT = b.sem("sSCT")
        b.wait("gpsimd", sGW, GW_TOTAL)
        b.wait("gpsimd", sVA, ROUTE_VA)
        for f in range(ST):
            b.op("gpsimd", lambda g, f=f: g.indirect_dma_start(
                out=cin, out_offset=IndirectOffsetOnAxis(ssc0[:, f:f + 1], 0),
                in_=src2bf[:, f * D:(f + 1) * D], in_offset=None), sSCT, 16)
            b.op("gpsimd", lambda g, f=f: g.indirect_dma_start(
                out=cin, out_offset=IndirectOffsetOnAxis(ssc1[:, f:f + 1], 0),
                in_=src2bf[:, f * D:(f + 1) * D], in_offset=None), sSCT, 16)
        SCT_ALL = sSCT.v
        b.wait("gpsimd", sSCT, SCT_ALL)
        pSRC2.close()
        pATT.close()

        pC2 = ExitStack()
        w1s = [b.sb("w1s0", [128, 8 * DFF], BF16, pC2), b.sb("w1s1", [128, 8 * DFF], BF16, pC2)]
        w2s = [b.sb("w2s0", [128, 16 * D], BF16, pC2), b.sb("w2s1", [128, 16 * D], BF16, pC2)]
        xfm = [b.sb("xfm0", [128, 8 * 512], BF16, pC2),
               b.sb("xfm1", [128, 8 * 512], BF16, pC2)]
        hfm = b.sb("hfm", [128, 16 * 512], BF16, pC2)
        outt = [b.sb("outt0", [128, D], BF16, pC2), b.sb("outt1", [128, D], BF16, pC2)]
        ph0 = b.ps("ph0", [128, 512], F32, pC2)
        ph1 = b.ps("ph1", [128, 512], F32, pC2)
        po0 = b.ps("po0", [128, 512], F32, pC2)
        po1 = b.ps("po1", [128, 512], F32, pC2)
        sWB = (b.sem("sWB0"), b.sem("sWB1"))
        sCI = (b.sem("sCI0"), b.sem("sCI1"))
        sCO = (b.sem("sCO0"), b.sem("sCO1"))
        wfree = {0: 0, 1: 0}
        xfree_c = {0: 0, 1: 0}
        ci_t = {0: 0, 1: 0}
        b.wait("gpsimd", sVA, woff_t)
        b.wait("sync", sSCT, SCT_ALL)
        for _e in ("gpsimd", "sync"):
            b.wait(_e, sS2[0], B5_SS[1])
            b.wait(_e, sS2[1], B5_SS[3])
        hfm_free = 0
        out_free = {0: 0, 1: 0}
        po_free = [0]
        for j in range(NBLK):
            par = j % 2
            b.wait("gpsimd", sPE, wfree[par])
            b.op("gpsimd", lambda g, j=j, par=par: g.indirect_dma_start(
                out=w1s[par][:], out_offset=None,
                in_=w1b, in_offset=IndirectOffsetOnAxis(widxu[:, j:j + 1], 0)), sWB[par], 16)
            b.op("gpsimd", lambda g, j=j, par=par: g.indirect_dma_start(
                out=w2s[par][:], out_offset=None,
                in_=w2b, in_offset=IndirectOffsetOnAxis(widxu[:, j:j + 1], 0)), sWB[par], 16)
            WB_T = sWB[par].v
            for half_b in range(CBLK // 512):
                hpar = (j * (CBLK // 512) + half_b) % 2
                rbase = j * CBLK + half_b * 512
                gi = j * (CBLK // 512) + half_b
                if gi == 0:
                    for c in range(8):
                        cld0 = b.dma("sync", xfm[0][:, c * 512:(c + 1) * 512],
                                     cin[0:512, c * 128:(c + 1) * 128], sCI[0],
                                     transpose=True)
                    ci_t[0] = cld0
                if gi + 1 < NBLK * (CBLK // 512):
                    npar = (gi + 1) % 2
                    nbase = (gi + 1) * 512
                    b.wait("sync", sPE, xfree_c[npar])
                    for c in range(8):
                        cldn = b.dma("sync", xfm[npar][:, c * 512:(c + 1) * 512],
                                     cin[nbase:nbase + 512, c * 128:(c + 1) * 128],
                                     sCI[npar], transpose=True)
                    ci_t[npar] = cldn
                b.wait("tensor", sCI[hpar], ci_t[hpar])
                b.wait("tensor", sWB[par], WB_T)
                b.wait("tensor", sSA)
                b.wait("scalar", sPE, hfm_free)
                hbank = {0: 0, 1: 0}
                for hc in range(16):
                    hb = hc % 2
                    ph = (ph0, ph1)[hb]
                    b.wait("tensor", sSA, hbank[hb])
                    for c in range(8):
                        th = b.op("tensor", lambda t, hc=hc, c=c, par=par, ph=ph,
                                  hpar=hpar: t.matmul(
                            ph[:, :], w1s[par][:, c * DFF + hc * 128: c * DFF + hc * 128 + 128],
                            xfm[hpar][:, c * 512:(c + 1) * 512], start=(c == 0), stop=(c == 7)),
                            sPE if c == 7 else None)
                    b.wait("scalar", sPE, th)
                    hbank[hb] = b.op("scalar", lambda s, hc=hc, ph=ph: s.activation(
                        hfm[:, hc * 512:(hc + 1) * 512], ph[:, :], AF.Gelu), sSA)
                xfree_c[hpar] = sPE.v
                GEL = sSA.v
                b.wait("tensor", sSA, GEL)
                for r in range(4):
                    opar = r % 2
                    b.wait("tensor", sCO[opar], out_free[opar])
                    b.wait("tensor", sVA, po_free[0])
                    for half, po in ((0, po0), (1, po1)):
                        for fc in range(16):
                            tw2 = b.op("tensor", lambda t, r=r, half=half, fc=fc, par=par, po=po: t.matmul(
                                po[:, :], hfm[:, fc * 512 + r * 128: fc * 512 + r * 128 + 128],
                                w2s[par][:, fc * D + half * 512: fc * D + half * 512 + 512],
                                start=(fc == 0), stop=(fc == 15)), sPE if fc == 15 else None)
                    b.wait("vector", sPE, tw2)
                    for half, po in ((0, po0), (1, po1)):
                        oev = b.op("vector", lambda v, half=half, po=po, opar=opar: v.tensor_copy(
                            outt[opar][:, half * 512:(half + 1) * 512], po[:, :]), sVA)
                    po_free[0] = oev
                    b.wait("sync", sVA, oev)
                    b.dma("sync", cout[rbase + r * 128: rbase + (r + 1) * 128, :],
                          outt[opar][:], sCO[opar])
                    out_free[opar] = sCO[opar].v
                hfm_free = sPE.v
            hfm_free = sPE.v
            wfree[par] = sPE.v
        CO_ALL = (sCO[0].v, sCO[1].v)
        pC2.close()

        pC3 = ExitStack()
        g0 = b.sb("g0", [128, 2 * D], BF16, pC3)
        g1 = b.sb("g1", [128, 2 * D], BF16, pC3)
        s2b3 = b.sb("s2b3", [128, 2 * D], BF16, pC3)
        x1b3 = b.sb("x1b3", [128, 2 * D], F32, pC3)
        src3 = b.sb("src3", [128, D], F32, pC3)
        sqb3 = b.sb("sqb3", [128, D], BF16, pC3)
        st6 = b.sb("st6", [128, 4], F32, pC3)
        outf = b.sb("outf", [128, 2 * D], F32, pC3)
        sGA = (b.sem("sGA0"), b.sem("sGA1"))
        sLD3 = (b.sem("sLD30"), b.sem("sLD31"))
        sOUTp = (b.sem("sOUT0"), b.sem("sOUT1"))
        b.wait("gpsimd", sCO[0], CO_ALL[0])
        b.wait("gpsimd", sCO[1], CO_ALL[1])
        b.wait("sync", sCO[0], CO_ALL[0])
        b.wait("sync", sCO[1], CO_ALL[1])
        gfree = {}
        outfree = {}
        for f in range(ST):
            par = f % 2
            mz = b.op("vector", lambda v, par=par: v.memset(g0[:, par * D:(par + 1) * D], 0.0), sVA)
            mz2 = b.op("vector", lambda v, par=par: v.memset(g1[:, par * D:(par + 1) * D], 0.0), sVA)
            b.wait("gpsimd", sVA, mz2)
            b.op("gpsimd", lambda g, f=f, par=par: g.indirect_dma_start(
                out=g0[:, par * D:(par + 1) * D], out_offset=None,
                in_=cout, in_offset=IndirectOffsetOnAxis(sc0u[:, f:f + 1], 0),
                compute_op=ALU.add), sGA[par], 16)
            ga = b.op("gpsimd", lambda g, f=f, par=par: g.indirect_dma_start(
                out=g1[:, par * D:(par + 1) * D], out_offset=None,
                in_=cout, in_offset=IndirectOffsetOnAxis(sc1u[:, f:f + 1], 0),
                compute_op=ALU.add), sGA[par], 16)
            if f >= 2:
                b.wait("sync", sVA, gfree[f - 2])
            l1 = b.dma("sync", s2b3[:, par * D:(par + 1) * D], s2h[f * 128:(f + 1) * 128, :], sLD3[par])
            l2 = b.dma("sync", x1b3[:, par * D:(par + 1) * D], x1h[f * 128:(f + 1) * 128, :], sLD3[par])
            b.wait("vector", sGA[par], ga)
            b.wait("vector", sLD3[par], l2)
            t1 = b.op("vector", lambda v, f=f, par=par: v.scalar_tensor_tensor(
                src3[:], g0[:, par * D:(par + 1) * D], w0c[:, f:f + 1],
                s2b3[:, par * D:(par + 1) * D], op0=ALU.mult, op1=ALU.add), sVA)
            b.wait("vector", sVA, t1)
            t2 = b.op("vector", lambda v, f=f, par=par: v.scalar_tensor_tensor(
                src3[:], g1[:, par * D:(par + 1) * D], w1c[:, f:f + 1],
                src3[:], op0=ALU.mult, op1=ALU.add), sVA)
            gfree[f] = t2
            b.wait("scalar", sVA, t2)
            b.op("scalar", lambda s: s.activation(sqb3[:], src3[:], AF.Square,
                                                  accum_out=st6[:, 0:1]), sSA)
            b.wait("scalar", sSA)
            b.op("scalar", lambda s: s.activation(st6[:, 1:2], st6[:, 0:1], AF.Ln,
                                                  scale=CINVD, bias=CEPS6), sSA)
            b.wait("scalar", sSA)
            b.op("scalar", lambda s: s.activation(st6[:, 2:3], st6[:, 1:2], AF.Exp,
                                                  scale=CNEGH), sSA)
            r6 = sSA.v
            b.wait("vector", sSA, r6)
            if f >= 2:
                b.wait("vector", sOUTp[par], outfree[f - 2])
            t3 = b.op("vector", lambda v, par=par: v.scalar_tensor_tensor(
                outf[:, par * D:(par + 1) * D], src3[:], st6[:, 2:3], c_n2[:],
                op0=ALU.mult, op1=ALU.mult), sVA)
            b.wait("vector", sVA, t3)
            t4 = b.op("vector", lambda v, par=par: v.tensor_add(
                outf[:, par * D:(par + 1) * D], outf[:, par * D:(par + 1) * D],
                x1b3[:, par * D:(par + 1) * D]), sVA)
            b.wait("sync", sVA, t4)
            b.dma("sync", out_ap[f * 128:(f + 1) * 128, :], outf[:, par * D:(par + 1) * D], sOUTp[par])
            outfree[f] = sOUTp[par].v
        b.wait("sync", sOUTp[0], sOUTp[0].v)
        b.wait("sync", sOUTp[1], sOUTp[1].v)
        pC3.close()
    if "B" in phases:
        phB.close()

    if "C" not in phases:
        devp = ExitStack()
        sDV = b.sem("sDV")
        tbuf = b.sb("tbuf", [128, D], F32, devp)
        for i in range(ST):
            t = b.dma("sync", tbuf[:], x1h[i * 128:(i + 1) * 128, :], sDV)
            b.wait("sync", sDV, t)
            st = b.dma("sync", out_ap[i * 128:(i + 1) * 128, :], tbuf[:], sOUT)
            b.wait("sync", sOUT, st)
        devp.close()

    with nc.Block() as block:
        @block.sync
        def _(e):
            for f in b.ops["sync"]:
                f(e)

        @block.gpsimd
        def _(e):
            for f in b.ops["gpsimd"]:
                f(e)

        @block.vector
        def _(e):
            for f in b.ops["vector"]:
                f(e)

        @block.scalar
        def _(e):
            for f in b.ops["scalar"]:
                f(e)

        @block.tensor
        def _(e):
            for f in b.ops["tensor"]:
                f(e)

    ctx.close()
    return nc


def make_aux(core_id):
    am = np.zeros((128, 128), np.float32)
    am[np.arange(128)[:, None] < np.arange(128)[None, :]] = -1e9
    cm = np.zeros((NCORES, 1), np.float32)
    cm[:core_id, 0] = 1.0
    return {
        "identf": np.eye(128, dtype=np.float32),
        "identb": np.eye(128, dtype=np.float32).astype(bfdt),
        "triub": np.triu(np.ones((128, 128), np.float32)).astype(bfdt),
        "amask": am,
        "onesrow": np.ones((1, 512), np.float32).astype(bfdt),
        "onesrow_f": np.ones((1, 128), np.float32),
        "onescol": np.ones((128, 1), np.float32).astype(bfdt),
        "coremask": cm,
        "thr16": (np.arange(16, dtype=np.float32) * CBLK).reshape(1, 16),
        "iota128": np.arange(128, dtype=np.float32).reshape(128, 1),
        "framp": (np.arange(ST, dtype=np.float32) * 256).reshape(1, ST),
        "js24": np.arange(NBLK, dtype=np.float32).reshape(1, NBLK),
    }


def kernel(L0_attn_x, L0_attn_Wq, L0_attn_Wk, L0_attn_Wv,
           L0_ffn_qw, L0_ffn_qb, L0_ffn_kw, L0_ffn_kb, L0_ffn_vw, L0_ffn_vb,
           L0_ffn_ow, L0_ffn_ob, L0_ffn_gate_w, L0_ffn_w1, L0_ffn_w2,
           L0_ffn_norm1_w, L0_ffn_norm2_w, _phases=("A", "B", "C"), _taps=()):
    x = np.ascontiguousarray(np.asarray(L0_attn_x, dtype=np.float32))
    shared = {
        "wqT": np.ascontiguousarray(np.asarray(L0_attn_Wq).T),
        "wkT": np.ascontiguousarray(np.asarray(L0_attn_Wk).T),
        "wvT": np.ascontiguousarray(np.asarray(L0_attn_Wv).T),
        "qw": np.ascontiguousarray(np.asarray(L0_ffn_qw)[:, HEAD_PERM]),
        "kw": np.ascontiguousarray(np.asarray(L0_ffn_kw)),
        "vw": np.ascontiguousarray(np.asarray(L0_ffn_vw)),
        "ow": np.ascontiguousarray(np.asarray(L0_ffn_ow)[HEAD_PERM, :]),
        "qb": np.ascontiguousarray(np.asarray(L0_ffn_qb)[HEAD_PERM].reshape(8, 128).T),
        "qbr": np.ascontiguousarray(np.asarray(L0_ffn_qb)[HEAD_PERM].reshape(1, D)),
        "kbr": np.ascontiguousarray(np.asarray(L0_ffn_kb).reshape(1, 256)),
        "kb": np.ascontiguousarray(np.asarray(L0_ffn_kb).reshape(2, 128).T),
        "vbr": np.ascontiguousarray(np.asarray(L0_ffn_vb).reshape(1, 256)),
        "obr": np.ascontiguousarray(np.asarray(L0_ffn_ob).reshape(1, D)),
        "gw": np.ascontiguousarray(np.asarray(L0_ffn_gate_w).reshape(8, 128, 8).transpose(1, 0, 2).reshape(128, 64)),
        "w1": np.ascontiguousarray(np.asarray(L0_ffn_w1).reshape(NEXP * D, DFF)),
        "w2": np.ascontiguousarray(np.asarray(L0_ffn_w2).reshape(NEXP * DFF, D)),
        "n1wb": np.ascontiguousarray(np.broadcast_to(np.asarray(L0_ffn_norm1_w), (128, D))),
        "n2wb": np.ascontiguousarray(np.broadcast_to(np.asarray(L0_ffn_norm2_w), (128, D))),
    }
    nc = build_program(phases=_phases, taps=_taps)
    in_maps = []
    for c in range(NCORES):
        m = dict(shared)
        m["x"] = np.ascontiguousarray(x[c])
        m.update(make_aux(c))
        in_maps.append(m)
    res = run_bass_kernel_spmd(nc, in_maps, core_ids=list(range(NCORES)))
    out = np.stack([np.asarray(res.results[c]["out"]) for c in range(NCORES)])
    if _taps:
        tapres = [{t: np.asarray(res.results[c]["tap_" + t]) for t in _taps} for c in range(NCORES)]
        return out, tapres
    return out


# version 16
# speedup vs baseline: 1.0122x; 1.0122x over previous
import numpy as np
import ml_dtypes
from contextlib import ExitStack

import concourse.bass as bass
import concourse.mybir as mybir
from concourse.bass_utils import run_bass_kernel_spmd
from concourse.bass_core_rust import IndirectOffsetOnAxis

F32 = mybir.dt.float32
BF16 = mybir.dt.bfloat16
I32 = mybir.dt.int32
U32 = mybir.dt.uint32
AF = mybir.ActivationFunctionType
ALU = mybir.AluOpType
AX = mybir.AxisListType
bfdt = ml_dtypes.bfloat16

S, D, ST = 4096, 1024, 32
NCORES = 8
NEXP, CAP = 8, 10240
DFF = 2048
CBLK = 1024
NBLK = NEXP + (2 * S) // CBLK
NCROWS = NBLK * CBLK


def configure(s, ncores=None, cap=None):
    global S, ST, NBLK, NCROWS, NCORES, CAP
    S, ST = s, s // 128
    NBLK = NEXP + (2 * S) // CBLK
    NCROWS = NBLK * CBLK
    if ncores is not None:
        NCORES = ncores
    if cap is not None:
        CAP = cap
NKV, NH, HDIM = 4, 16, 64
GQA_C = 32.0
QTL_ENG = "sync"
BIGPOS = 1.0e9

_HA = [0, 1, 2, 3, 8, 9, 10, 11]
_HB = [4, 5, 6, 7, 12, 13, 14, 15]
HEAD_PERM = []
for _j in range(8):
    HEAD_PERM += list(range(_HA[_j] * 64, _HA[_j] * 64 + 64))
    HEAD_PERM += list(range(_HB[_j] * 64, _HB[_j] * 64 + 64))
HEAD_PERM = np.array(HEAD_PERM)
HEAD_SLOT = {}
for _j in range(8):
    HEAD_SLOT[_HA[_j]] = (_j, 0)
    HEAD_SLOT[_HB[_j]] = (_j, 64)

ENGS = ("sync", "gpsimd", "vector", "scalar", "tensor")


class Sem:
    def __init__(self, nc, ctx, name):
        self.h = ctx.enter_context(nc.semaphore(name))
        self.v = 0


class B:
    def __init__(self, nc, ctx):
        self.nc = nc
        self.ctx = ctx
        self.ops = {e: [] for e in ENGS}
        self.sems = {}

    def sem(self, name):
        if name not in self.sems:
            self.sems[name] = Sem(self.nc, self.ctx, name)
        return self.sems[name]

    def emit(self, eng, fn):
        self.ops[eng].append(fn)

    def wait(self, eng, sem, val=None):
        v = sem.v if val is None else val
        if v > 0:
            self.emit(eng, lambda e, _s=sem.h, _v=v: e.wait_ge(_s, _v))

    def op(self, eng, fn, sem=None, n=1):
        if sem is not None:
            sem.v += n
            tgt = sem.v
            self.emit(eng, lambda e, _f=fn, _s=sem.h, _n=n: _f(e).then_inc(_s, _n))
            return tgt
        self.emit(eng, fn)
        return None

    def dma(self, eng, out, in_, sem, **kw):
        return self.op(eng, lambda e: e.dma_start(out=out, in_=in_, **kw), sem, 16)

    def sb(self, name, shape, dtype=F32, scope=None):
        return (scope or self.ctx).enter_context(self.nc.sbuf_tensor(name, shape, dtype))

    def ps(self, name, shape, dtype=F32, scope=None):
        return (scope or self.ctx).enter_context(self.nc.psum_tensor(name, shape, dtype))


def cdiv(a, b):
    return (a + b - 1) // b


def build_program(phases=("A", "B", "C"), taps=()):
    nc = bass.Bass("TRN2", target_bir_lowering=False, debug=False)
    ctx = ExitStack()
    b = B(nc, ctx)

    def inp(name, shape, dtype=F32):
        return nc.dram_tensor(name, shape, dtype, kind="ExternalInput").ap()

    x_in = inp("x", [S, D])
    wqT = inp("wqT", [D, D]); wkT = inp("wkT", [D, D]); wvT = inp("wvT", [D, D])
    qw = inp("qw", [D, D]); kw_ = inp("kw", [D, 256]); vw = inp("vw", [D, 256])
    ow = inp("ow", [D, D])
    qb = inp("qb", [128, 8]); kb = inp("kb", [128, 2]); vbr = inp("vbr", [1, 256])
    qbr = inp("qbr", [1, D]); kbr = inp("kbr", [1, 256])
    obr = inp("obr", [1, D])
    gw_in = inp("gw", [128, 64])
    w1_in = inp("w1", [NEXP * D, DFF]); w2_in = inp("w2", [NEXP * DFF, D])
    n1wb = inp("n1wb", [128, D]); n2wb = inp("n2wb", [128, D])
    identf = inp("identf", [128, 128])
    identb = inp("identb", [128, 128], BF16)
    triub = inp("triub", [128, 128], BF16)
    amask = inp("amask", [128, 128])
    onesrow = inp("onesrow", [1, 512], BF16)
    onesrow_f = inp("onesrow_f", [1, 128])
    onescol = inp("onescol", [128, 1], BF16)
    coremask = inp("coremask", [NCORES, 1])
    thr16 = inp("thr16", [1, 16])
    js24 = inp("js24", [1, NBLK])
    iota128 = inp("iota128", [128, 1])
    framp = inp("framp", [1, ST])

    out_ap = nc.dram_tensor("out", [S, D], F32, kind="ExternalOutput").ap()

    def dram(name, shape, dtype=F32):
        return nc.dram_tensor(name, shape, dtype, kind="Internal").ap()

    qsp = dram("qsp", [8, 128, S], BF16)
    x1h = dram("x1h", [S, D])
    s2h = dram("s2h", [S, D], BF16)
    w1b = dram("w1b", [NEXP * 128, 8 * DFF], BF16)
    w2b = dram("w2b", [NEXP * 128, 16 * D], BF16)
    cin = dram("cin", [NCROWS + 2 * S, D], BF16)
    cout = dram("cout", [NCROWS + 1, D], BF16)
    ag_in = dram("ag_in", [1, 8]); ag_out = dram("ag_out", [NCORES, 8])

    TAPS = {"x1": [S, D], "src2": [S, D], "probs": [128, 256],
            "route": [128, 256], "cnt": [1, 24]}
    tap_map = {t: nc.dram_tensor("tap_" + t, TAPS[t], F32, kind="ExternalOutput").ap()
               for t in taps}

    c_idb = b.sb("c_idb", [128, 128], BF16)

    sLD = b.sem("sLD"); sGW = b.sem("sGW"); sPH = b.sem("sPH")
    sDX = b.sem("sDX"); sVA = b.sem("sVA"); sSA = b.sem("sSA")
    sPE = b.sem("sPE"); sQS = b.sem("sQS"); sX1 = b.sem("sX1")
    sGP = b.sem("sGP"); sCC = b.sem("sCC"); sOUT = b.sem("sOUT")

    sLDg = b.sem("sLDg")
    b.dma("gpsimd", c_idb[:], identb, sLDg)

    c_sc = b.sb("c_sc", [128, 8])
    for col, val in [(0, 1.0 / D), (1, 1e-5), (2, 1e-6), (3, -0.5), (4, -GQA_C),
                     (5, 1.0 / 32.0), (6, 1.0 / 8.0)]:
        b.op("vector", lambda v, col=col, val=val: v.memset(c_sc[:, col:col + 1], val), sVA)
    CONSTV = sVA.v
    for e in ("scalar", "tensor", "gpsimd", "sync"):
        b.wait(e, sVA, CONSTV)
    CINVD = c_sc[:, 0:1]; CEPS5 = c_sc[:, 1:2]; CEPS6 = c_sc[:, 2:3]
    CNEGH = c_sc[:, 3:4]; CNEGC = c_sc[:, 4:5]; CINV32 = c_sc[:, 5:6]
    CINV8 = c_sc[:, 6:7]
    for e in ENGS:
        b.wait(e, sLD, sLD.v)
        b.wait(e, sLDg, sLDg.v)

    if "C" in phases:
        for e in range(NEXP):
            b.dma("gpsimd",
                  w1b[e * 128:(e + 1) * 128, :].rearrange("p (c h) -> p c h", c=8),
                  w1_in[e * D:(e + 1) * D, :].rearrange("(c p) h -> p c h", p=128), sGW)
            b.dma("gpsimd",
                  w2b[e * 128:(e + 1) * 128, :].rearrange("p (c h) -> p c h", c=16),
                  w2_in[e * DFF:(e + 1) * DFF, :].rearrange("(c p) h -> p c h", p=128), sGW)
        zb = b.sb("zb", [128, D], BF16)
        b.op("gpsimd", lambda g: g.memset(zb[:], 0.0), sGP)
        b.wait("gpsimd", sGP, sGP.v)
        for r in range(NCROWS // 128):
            b.dma("gpsimd", cin[r * 128:(r + 1) * 128, :], zb[:], sGW)
        b.dma("gpsimd", cout[NCROWS:NCROWS + 1, :], zb[0:1, :], sGW)
    GW_TOTAL = sGW.v

    def emit_ln(xsrc, xc, sqscr, stat, mcol, rcol, ln_out, eps_ap, ln_prev_done):
        b.op("vector", lambda v: v.reduce_sum(stat[:, 0:1], xsrc, axis=AX.X), sVA)
        b.wait("vector", sVA)
        b.op("vector", lambda v: v.tensor_scalar_mul(mcol, stat[:, 0:1], 1.0 / D), sVA)
        b.wait("vector", sVA)
        b.op("vector", lambda v: v.tensor_scalar(xc, xsrc, mcol, None, op0=ALU.subtract), sVA)
        xc_t = sVA.v
        b.wait("scalar", sVA, xc_t)
        b.op("scalar", lambda s: s.activation(sqscr, xc, AF.Square, accum_out=stat[:, 1:2]), sSA)
        b.wait("scalar", sSA)
        b.op("scalar", lambda s: s.activation(stat[:, 2:3], stat[:, 1:2], AF.Ln, scale=CINVD, bias=eps_ap), sSA)
        b.wait("scalar", sSA)
        b.op("scalar", lambda s: s.activation(rcol, stat[:, 2:3], AF.Exp, scale=CNEGH), sSA)
        rstd_t = sSA.v
        b.wait("vector", sSA, rstd_t)
        ln_t = b.op("vector", lambda v: v.tensor_scalar_mul(ln_out, xc, rcol), sVA)
        return ln_t, xc_t

    if "A" in phases:
        phA = ExitStack()
        kfm = b.sb("kfm", [128, 8 * S], BF16, phA)
        vtm = b.sb("vtm", [128, ST * D], BF16, phA)

        p1 = ExitStack()
        xbuf = b.sb("xbuf", [128, 2 * D], F32, p1)
        st1 = b.sb("st1", [128, 4], F32, p1)
        m1s = b.sb("m1s", [128, ST], F32, p1)
        r1s = b.sb("r1s", [128, ST], F32, p1)
        lntm = b.sb("lntm", [128, D], BF16, p1)
        lnf = b.sb("lnf", [128, 8 * 512], BF16, p1)
        qwin = b.sb("qwin", [128, 8 * 512], BF16, p1)
        wq_s = b.sb("wq_s", [128, 8 * D], BF16, p1)
        wk_s = b.sb("wk_s", [128, 8 * D], BF16, p1)
        wv_s = b.sb("wv_s", [128, 8 * D], BF16, p1)
        ptA = b.ps("ptA", [128, 128], BF16, p1)
        ptB = b.ps("ptB", [128, 128], BF16, p1)
        pj0 = b.ps("pj0", [128, 512], F32, p1)
        pj1 = b.ps("pj1", [128, 512], F32, p1)
        pj2 = b.ps("pj2", [128, 512], F32, p1)
        pj3 = b.ps("pj3", [128, 512], F32, p1)

        for dst, src in [(wq_s, wqT), (wk_s, wkT), (wv_s, wvT)]:
            for c in range(8):
                b.dma("gpsimd", dst[:, c * D:(c + 1) * D],
                      src[c * 128:(c + 1) * 128, :], sLDg)
        for e in ENGS:
            b.wait(e, sLD, sLD.v)
            b.wait(e, sLDg, sLDg.v)

        tr_evac = []
        xfree = {}
        xload = []
        sDXa = (b.sem("sDXa0"), b.sem("sDXa1"))
        qspill_mark = 0
        aspills = []
        for i in range(ST):
            slot = i % 2
            xap = xbuf[:, slot * D:(slot + 1) * D]
            if i >= 2:
                b.wait("sync", sVA, xfree[i - 2])
            xload.append(b.dma("sync", xap, x_in[i * 128:(i + 1) * 128, :], sDXa[slot]))
            b.wait("vector", sDXa[slot], xload[i])
            if i >= 1:
                b.wait("vector", sSA)
            b.wait("scalar", sPE)
            ln_t, xc_t = emit_ln(xap, xap, lntm[:], st1, m1s[:, i:i + 1],
                                 r1s[:, i:i + 1], lntm[:], CEPS5, None)
            xfree[i] = ln_t
            w = i % 4
            b.wait("tensor", sVA, ln_t)
            if i % 4 == 0 and i > 0:
                b.wait("scalar", sQS, qspill_mark)
            for c in range(8):
                pt = (ptA, ptB)[c % 2]
                if len(tr_evac) >= 2:
                    b.wait("tensor", sVA, tr_evac[-2])
                tp = b.op("tensor", lambda t, pt=pt, c=c: t.transpose(
                    pt[:, :], lntm[:, c * 128:(c + 1) * 128], c_idb[:]), sPE)
                b.wait("vector", sPE, tp)
                tr_evac.append(b.op("vector", lambda v, pt=pt, c=c, w=w: v.tensor_copy(
                    lnf[:, c * 512 + w * 128: c * 512 + w * 128 + 128], pt[:, :]), sVA))
            if w == 3:
                i4 = i // 4
                b.wait("tensor", sVA, tr_evac[-1])
                ev_q = ev_k = None
                for oc in range(8):
                    if ev_q is not None:
                        b.wait("tensor", sSA, ev_q)
                    for c in range(8):
                        mm = lambda t, oc=oc, c=c: t.matmul(
                            pj0[:, :], wq_s[:, c * D + oc * 128: c * D + oc * 128 + 128],
                            lnf[:, c * 512:(c + 1) * 512], start=(c == 0), stop=(c == 7))
                        tq = b.op("tensor", mm, sPE if c == 7 else None)
                    b.wait("scalar", sPE, tq)
                    ev_q = b.op("scalar", lambda s, oc=oc: s.activation(
                        qwin[:, oc * 512:(oc + 1) * 512], pj0[:, :], AF.Copy,
                        scale=CINV32), sSA)
                    if ev_k is not None:
                        b.wait("tensor", sSA, ev_k)
                    for c in range(8):
                        mm = lambda t, oc=oc, c=c: t.matmul(
                            pj1[:, :], wk_s[:, c * D + oc * 128: c * D + oc * 128 + 128],
                            lnf[:, c * 512:(c + 1) * 512], start=(c == 0), stop=(c == 7))
                        tk = b.op("tensor", mm, sPE if c == 7 else None)
                    b.wait("scalar", sPE, tk)
                    ev_k = b.op("scalar", lambda s, oc=oc, i4=i4: s.activation(
                        kfm[:, oc * S + i4 * 512: oc * S + i4 * 512 + 512], pj1[:, :], AF.Copy), sSA)
                ev_v = {0: None, 1: None}
                for sc in range(4):
                    st_idx = i4 * 4 + sc
                    for half, pj in ((0, pj2), (1, pj3)):
                        if ev_v[half] is not None:
                            b.wait("tensor", sVA, ev_v[half])
                        for c in range(8):
                            mm = lambda t, sc=sc, half=half, pj=pj, c=c: t.matmul(
                                pj[:, :], lnf[:, c * 512 + sc * 128: c * 512 + sc * 128 + 128],
                                wv_s[:, c * D + half * 512: c * D + half * 512 + 512],
                                start=(c == 0), stop=(c == 7))
                            tv = b.op("tensor", mm, sPE if c == 7 else None)
                        b.wait("vector", sPE, tv)
                        ev_v[half] = b.op("vector", lambda v, st_idx=st_idx, half=half, pj=pj: v.tensor_copy(
                            vtm[:, st_idx * D + half * 512: st_idx * D + half * 512 + 512], pj[:, :]), sVA)
                b.wait("sync", sSA, ev_k)
                for oc in range(8):
                    b.dma("sync", qsp[oc, :, i4 * 512:(i4 + 1) * 512],
                          qwin[:, oc * 512:(oc + 1) * 512], sQS)
                qspill_mark = sQS.v
                aspills.append(qspill_mark)
        KFM_DONE = sSA.v
        VTM_DONE = sVA.v
        p1.close()

        if "A1" in phases:
            for e in ("tensor", "sync"):
                b.wait(e, sSA, KFM_DONE)
                b.wait(e, sVA, VTM_DONE)
                b.wait(e, sQS)
            sDV1 = b.sem("sDV1")
            dv1 = ExitStack()
            tb1 = b.sb("tb1", [128, D], F32, dv1)
            for i in range(ST):
                t = b.dma("sync", tb1[:], x_in[i * 128:(i + 1) * 128, :], sDV1)
                b.wait("sync", sDV1, t)
                st = b.dma("sync", out_ap[i * 128:(i + 1) * 128, :], tb1[:], sOUT)
                b.wait("sync", sOUT, st)
            dv1.close()

        if "A1" in phases:
            phA.close()
            with nc.Block() as block:
                @block.sync
                def _(e):
                    for f in b.ops["sync"]:
                        f(e)

                @block.gpsimd
                def _(e):
                    for f in b.ops["gpsimd"]:
                        f(e)

                @block.vector
                def _(e):
                    for f in b.ops["vector"]:
                        f(e)

                @block.scalar
                def _(e):
                    for f in b.ops["scalar"]:
                        f(e)

                @block.tensor
                def _(e):
                    for f in b.ops["tensor"]:
                        f(e)
            ctx.close()
            return nc
        p2 = ExitStack()
        for e in ("tensor", "sync"):
            b.wait(e, sSA, KFM_DONE)
            b.wait(e, sVA, VTM_DONE)
            b.wait(e, sQS)
        qtl = b.sb("qtl", [128, 3 * 1024], BF16, p2)
        xres = b.sb("xres", [128, 2 * D], F32, p2)
        srow = b.sb("srow", [128, S], F32, p2)
        prow = b.sb("prow", [128, 2 * S], BF16, p2)
        ptsb = b.sb("ptsb", [128, 3 * 128], BF16, p2)
        x1t = b.sb("x1t", [128, 2 * D], F32, p2)
        st2 = b.sb("st2", [128, 8], F32, p2)
        c_amask = b.sb("c_amask", [128, 128], F32, p2)
        sAM = b.sem("sAM")
        b.dma("sync", c_amask[:], amask, sAM)
        b.wait("vector", sAM, sAM.v)
        psc0 = b.ps("psc0", [128, 512], F32, p2)
        psc1 = b.ps("psc1", [128, 512], F32, p2)
        psc2 = b.ps("psc2", [128, 512], F32, p2)
        ppt0 = b.ps("ppt0", [128, 128], BF16, p2)
        ppt1 = b.ps("ppt1", [128, 128], BF16, p2)
        ppt2 = b.ps("ppt2", [128, 128], BF16, p2)
        ppv0 = b.ps("ppv0", [128, 512], F32, p2)
        ppv1 = b.ps("ppv1", [128, 512], F32, p2)

        sDXb = (b.sem("sDXb0"), b.sem("sDXb1"))
        sDXg = (b.sem("sDXg0"), b.sem("sDXg1"), b.sem("sDXg2"))
        sX1p = (b.sem("sX1p0"), b.sem("sX1p1"))
        bank_free = {0: 0, 1: 0, 2: 0}
        x1_done = {}
        sc_done = {}
        tr_marks = {}
        exp_m = {}
        rl_m = {}
        qld_m = {}
        xld_m = {}
        pendA = [None]

        def emit_qld(ti):
            if ti >= ST:
                return
            sl = ti % 3
            if ti >= 3:
                b.wait(QTL_ENG, sPE, sc_done[ti - 3])
            b.wait(QTL_ENG, sQS, aspills[ti // 4])
            for oc in range(8):
                qld_m[ti] = b.dma(QTL_ENG,
                                  qtl[:, sl * 1024 + oc * 128: sl * 1024 + oc * 128 + 128],
                                  qsp[oc, :, ti * 128:(ti + 1) * 128], sDXg[sl])

        def emit_tailA(ti):
            slot = ti % 2
            psl = ti % 2
            sb4 = 4 * (ti % 2)
            b.wait("tensor", sSA, exp_m[ti])
            if ti > 0:
                b.wait("tensor", sVA, x1_done[ti - 1])
            ptev = []

            def emit_pv(j, ti=ti):
                b.wait("tensor", sVA, ptev[j])
                for half, ppv in ((0, ppv0), (1, ppv1)):
                    mm = lambda t, j=j, half=half, ppv=ppv, ti=ti: t.matmul(
                        ppv[:, :], ptsb[:, (j % 3) * 128:(j % 3) * 128 + 128],
                        vtm[:, j * D + half * 512: j * D + half * 512 + 512],
                        start=(j == 0), stop=(j == ti))
                    return_t = b.op("tensor", mm, sPE if (j == ti and half == 1) else None)
                return return_t

            for j in range(ti + 1):
                pt = (ppt0, ppt1, ppt2)[j % 3]
                if j >= 3:
                    b.wait("tensor", sVA, ptev[j - 3])
                tp = b.op("tensor", lambda t, j=j, pt=pt, psl=psl: t.transpose(
                    pt[:, :], prow[:, psl * S + j * 128: psl * S + (j + 1) * 128],
                    c_idb[:]), sPE)
                b.wait("vector", sPE, tp)
                ptev.append(b.op("vector", lambda v, j=j, pt=pt: v.tensor_copy(
                    ptsb[:, (j % 3) * 128:(j % 3) * 128 + 128], pt[:, :]), sVA))
                if j >= 1:
                    emit_pv(j - 1)
            pv_t = emit_pv(ti)
            tr_marks[ti] = sPE.v
            b.wait("vector", sPE, pv_t)
            b.wait("vector", sVA, rl_m[ti])
            b.wait("vector", sDXb[slot], xld_m[ti])
            b.wait("vector", sX1p[slot], sX1p[slot].v)
            for half, ppv in ((0, ppv0), (1, ppv1)):
                x1_done[ti] = b.op("vector", lambda v, slot=slot, half=half, ppv=ppv,
                                   sb4=sb4: v.scalar_tensor_tensor(
                    x1t[:, slot * D + half * 512: slot * D + half * 512 + 512],
                    ppv[:, :], st2[:, sb4 + 3:sb4 + 4],
                    xres[:, slot * D + half * 512: slot * D + half * 512 + 512],
                    op0=ALU.mult, op1=ALU.add), sVA)
            b.wait("sync", sVA, x1_done[ti])
            b.dma("sync", x1h[ti * 128:(ti + 1) * 128, :],
                  x1t[:, slot * D:(slot + 1) * D], sX1p[slot])

        emit_qld(0)
        emit_qld(1)
        for qi in range(ST):
            slot = qi % 2
            qsl3 = qi % 3
            sb4 = 4 * (qi % 2)
            emit_qld(qi + 2)
            if qi >= 2:
                b.wait("sync", sVA, x1_done[qi - 2])
            xld_m[qi] = b.dma("sync", xres[:, slot * D:(slot + 1) * D],
                              x_in[qi * 128:(qi + 1) * 128, :], sDXb[slot])
            KW = cdiv((qi + 1) * 128, 512)
            Wq = (qi + 1) * 128
            if "P2Z" in phases:
                b.wait("vector", sDXb[slot], xld_m[qi])
                b.wait("vector", sX1p[slot], sX1p[slot].v)
                for half in range(2):
                    x1_done[qi] = b.op("vector", lambda v, slot=slot, half=half: v.tensor_copy(
                        x1t[:, slot * D + half * 512: slot * D + half * 512 + 512],
                        xres[:, slot * D + half * 512: slot * D + half * 512 + 512]), sVA)
                b.wait("sync", sVA, x1_done[qi])
                b.dma("sync", x1h[qi * 128:(qi + 1) * 128, :], x1t[:, slot * D:(slot + 1) * D], sX1p[slot])
                continue
            b.wait("tensor", sDXg[qsl3], qld_m[qi])
            if qi >= 1:
                b.wait("scalar", sSA, exp_m[qi - 1])
            if qi >= 2:
                b.wait("vector", sVA, x1_done[qi - 2])
            for kwi in range(KW):
                bk = kwi % 3
                psc = (psc0, psc1, psc2)[bk]
                b.wait("tensor", sSA, bank_free[bk])
                for c in range(8):
                    mm = lambda t, kwi=kwi, c=c, qsl3=qsl3, psc=psc: t.matmul(
                        psc[:, :], qtl[:, qsl3 * 1024 + c * 128: qsl3 * 1024 + c * 128 + 128],
                        kfm[:, c * S + kwi * 512: c * S + kwi * 512 + 512],
                        start=(c == 0), stop=(c == 7))
                    tm = b.op("tensor", mm, sPE if c == 7 else None)
                wlim = min(512, Wq - kwi * 512)
                b.wait("scalar", sPE, tm)
                ev = b.op("scalar", lambda s, kwi=kwi, psc=psc, wlim=wlim: s.activation(
                    srow[:, kwi * 512: kwi * 512 + wlim], psc[:, 0:wlim], AF.Copy), sSA)
                bank_free[bk] = ev
                b.wait("vector", sSA, ev)
                b.op("vector", lambda v, kwi=kwi, wlim=wlim, sb4=sb4: v.reduce_max(
                    st2[:, sb4 + 3:sb4 + 4], srow[:, kwi * 512: kwi * 512 + wlim],
                    axis=AX.X), sVA)
                b.wait("vector", sVA)
                if kwi == 0:
                    b.op("vector", lambda v, sb4=sb4: v.tensor_copy(
                        st2[:, sb4:sb4 + 1], st2[:, sb4 + 3:sb4 + 4]), sVA)
                else:
                    b.op("vector", lambda v, sb4=sb4: v.tensor_max(
                        st2[:, sb4:sb4 + 1], st2[:, sb4:sb4 + 1], st2[:, sb4 + 3:sb4 + 4]), sVA)
            sc_done[qi] = sPE.v
            b.wait("vector", sSA)
            b.op("vector", lambda v, qi=qi: v.tensor_add(
                srow[:, qi * 128: qi * 128 + 128], srow[:, qi * 128: qi * 128 + 128],
                c_amask[:]), sVA)
            b.wait("vector", sVA)
            b.op("vector", lambda v, sb4=sb4: v.tensor_scalar_mul(
                st2[:, sb4 + 1:sb4 + 2], st2[:, sb4:sb4 + 1], -1.0), sVA)
            nrm_t = sVA.v
            b.wait("scalar", sVA, nrm_t)
            if qi >= 2:
                b.wait("scalar", sPE, tr_marks[qi - 2])
            b.op("scalar", lambda s, Wq=Wq, qi=qi, sb4=sb4: s.activation(
                prow[:, (qi % 2) * S: (qi % 2) * S + Wq], srow[:, 0:Wq], AF.Exp,
                bias=st2[:, sb4 + 1:sb4 + 2], accum_out=st2[:, sb4 + 2:sb4 + 3]), sSA)
            exp_m[qi] = sSA.v
            b.wait("vector", sSA, exp_m[qi])
            b.op("vector", lambda v, sb4=sb4: v.reciprocal(
                st2[:, sb4 + 3:sb4 + 4], st2[:, sb4 + 2:sb4 + 3]), sVA)
            rl_m[qi] = sVA.v
            if pendA[0] is not None:
                emit_tailA(pendA[0])
            pendA[0] = qi
        if pendA[0] is not None and "P2Z" not in phases:
            emit_tailA(pendA[0])
        p2.close()
        phA.close()

        X1_ALL = (sX1p[0], sX1p[0].v, sX1p[1], sX1p[1].v)
        if "x1" in tap_map:
            b.wait("sync", sX1p[0], sX1p[0].v)
            b.wait("sync", sX1p[1], sX1p[1].v)
            tb = b.sb("tapb", [128, D])
            for i in range(ST):
                t = b.dma("sync", tb[:], x1h[i * 128:(i + 1) * 128, :], sDX)
                b.wait("sync", sDX, t)
                b.dma("sync", tap_map["x1"][i * 128:(i + 1) * 128, :], tb[:], sX1)

        b.wait("sync", sX1p[0], X1_ALL[1])
        b.wait("sync", sX1p[1], X1_ALL[3])
        b.op("sync", lambda e: e.nop(), sPH)
        for e in ("vector", "scalar", "tensor", "gpsimd"):
            b.wait(e, sPH, sPH.v)

    if "B" in phases:
        phB = ExitStack()
        c_idf = b.sb("c_idf", [128, 128], F32, phB)
        c_triu = b.sb("c_triu", [128, 128], BF16, phB)
        c_or = b.sb("c_or", [1, 512], BF16, phB)
        c_qbr = b.sb("c_qbr", [1, D], BF16, phB)
        c_kbr = b.sb("c_kbr", [1, 256], BF16, phB)
        c_orf = b.sb("c_orf", [1, 128], F32, phB)
        c_oc = b.sb("c_oc", [128, 1], BF16, phB)
        c_cm = b.sb("c_cm", [NCORES, 1], F32, phB)
        c_thr = b.sb("c_thr", [1, 16], F32, phB)
        c_js = b.sb("c_js", [1, NBLK], F32, phB)
        c_iota = b.sb("c_iota", [128, 1], F32, phB)
        c_qb = b.sb("c_qb", [128, 8], F32, phB)
        c_kb = b.sb("c_kb", [128, 2], F32, phB)
        c_vbr = b.sb("c_vbr", [1, 256], BF16, phB)
        c_obr = b.sb("c_obr", [1, D], BF16, phB)
        c_gw = b.sb("c_gw", [128, 64], F32, phB)
        c_n1 = b.sb("c_n1", [128, D], F32, phB)
        c_n2 = b.sb("c_n2", [128, D], F32, phB)
        m2s = b.sb("m2s", [128, ST], F32, phB)
        r2s = b.sb("r2s", [128, ST], F32, phB)
        logits = b.sb("logits", [128, ST * 8], F32, phB)
        probs = b.sb("probs", [128, ST * 8], F32, phB)
        w0c = b.sb("w0c", [128, ST], F32, phB)
        w1c = b.sb("w1c", [128, ST], F32, phB)
        wofff = b.sb("wofff", [1, NBLK], F32, phB)
        woffi = b.sb("woffi", [1, NBLK], I32, phB)
        wobase = b.sb("wobase", [1, 2 * NBLK], F32, phB)
        widxf = b.sb("widxf", [128, NBLK], F32, phB)
        widxu = b.sb("widxu", [128, NBLK], U32, phB)
        wtmp = b.sb("wtmp", [128, NBLK], F32, phB)
        sc0u = b.sb("sc0u", [128, ST], U32, phB)
        sc1u = b.sb("sc1u", [128, ST], U32, phB)
        ssc0 = b.sb("ssc0", [128, ST], U32, phB)
        ssc1 = b.sb("ssc1", [128, ST], U32, phB)
        c_framp = b.sb("c_framp", [1, ST], F32, phB)
        fbb = b.sb("fbb", [128, ST], F32, phB)
        sLDB = b.sem("sLDB")
        for dst, src in [(c_idf, identf), (c_cm, coremask), (c_thr, thr16),
                         (c_js, js24), (c_qb, qb), (c_kb, kb), (c_gw, gw_in),
                         (c_n1, n1wb), (c_n2, n2wb), (c_orf, onesrow_f),
                         (c_iota, iota128), (c_framp, framp)]:
            b.dma("sync", dst[:], src, sLDB)
        sLDBg = b.sem("sLDBg")
        for dst, src in [(c_triu, triub), (c_or, onesrow), (c_oc, onescol),
                         (c_vbr, vbr), (c_obr, obr), (c_qbr, qbr), (c_kbr, kbr)]:
            b.dma("gpsimd", dst[:], src, sLDBg)

        pATT = ExitStack()
        attc = b.sb("attc", [128, ST * D], BF16, pATT)
        pQKV = ExitStack()
        k2f = b.sb("k2f", [128, 4 * S], BF16, pQKV)
        v2t = b.sb("v2t", [128, ST * 260], BF16, pQKV)

        pB1 = ExitStack()
        x1b = b.sb("x1b", [128, 2 * D], F32, pB1)
        st3 = b.sb("st3", [128, 4], F32, pB1)
        ln2tm = b.sb("ln2tm", [128, D], BF16, pB1)
        lnf2 = b.sb("lnf2", [128, 8 * 512], BF16, pB1)
        qw_s = b.sb("qw_s2", [128, 8 * D], BF16, pB1)
        qwin2 = b.sb("qwin2", [128, 8 * 512], BF16, pB1)
        kw_s = b.sb("kw_s2", [128, 8 * 256], BF16, pB1)
        vw_s = b.sb("vw_s2", [128, 8 * 256], BF16, pB1)
        ptA2 = b.ps("ptA2", [128, 128], BF16, pB1)
        ptB2 = b.ps("ptB2", [128, 128], BF16, pB1)
        pq0 = b.ps("pq0", [128, 512], F32, pB1)
        pq1 = b.ps("pq1", [128, 512], F32, pB1)
        pq2 = b.ps("pq2", [128, 256], F32, pB1)
        for c in range(8):
            b.dma("gpsimd", qw_s[:, c * D:(c + 1) * D], qw[c * 128:(c + 1) * 128, :], sLDBg)
            b.dma("gpsimd", kw_s[:, c * 256:(c + 1) * 256], kw_[c * 128:(c + 1) * 128, :], sLDBg)
            b.dma("gpsimd", vw_s[:, c * 256:(c + 1) * 256], vw[c * 128:(c + 1) * 128, :], sLDBg)
        for e in ENGS:
            b.wait(e, sLDB, sLDB.v)
            b.wait(e, sLDBg, sLDBg.v)
        vones = b.op("vector", lambda v: v.memset(
            bass.AP(v2t.ap().tensor, 64, [[260 * ST, 128], [260, ST], [65, 4], [1, 1]]), 1.0), sVA)
        kz = b.op("vector", lambda v: v.memset(k2f[:, :], 0.0), sVA)
        b.wait("scalar", sVA, kz)

        sDXc = (b.sem("sDXc0"), b.sem("sDXc1"))
        sQS2 = b.sem("sQS2")
        qspill2_mark = 0
        tr_ev2 = []
        x1free = {}
        x1ld = []
        for i in range(ST):
            slot = i % 2
            xap = x1b[:, slot * D:(slot + 1) * D]
            if i >= 2:
                b.wait("sync", sVA, x1free[i - 2])
            bsrc = x_in if "BX" in phases else x1h
            x1ld.append(b.dma("sync", xap, bsrc[i * 128:(i + 1) * 128, :], sDXc[slot]))
            b.wait("vector", sDXc[slot], x1ld[i])
            if i >= 1:
                b.wait("vector", sSA)
            b.wait("scalar", sPE)
            ln_t, xc_t = emit_ln(xap, xap, ln2tm[:], st3, m2s[:, i:i + 1],
                                 r2s[:, i:i + 1], ln2tm[:], CEPS5, None)
            x1free[i] = ln_t
            w = i % 4
            b.wait("tensor", sVA, ln_t)
            if i % 4 == 0 and i > 0:
                b.wait("scalar", sQS2, qspill2_mark)
            for c in range(8):
                pt = (ptA2, ptB2)[c % 2]
                if len(tr_ev2) >= 2:
                    b.wait("tensor", sVA, tr_ev2[-2])
                tp = b.op("tensor", lambda t, pt=pt, c=c: t.transpose(
                    pt[:, :], ln2tm[:, c * 128:(c + 1) * 128], c_idb[:]), sPE)
                b.wait("vector", sPE, tp)
                tr_ev2.append(b.op("vector", lambda v, pt=pt, c=c, w=w: v.tensor_copy(
                    lnf2[:, c * 512 + w * 128: c * 512 + w * 128 + 128], pt[:, :]), sVA))
            if w == 3:
                i4 = i // 4
                b.wait("tensor", sVA, tr_ev2[-1])
                ev_q = ev_k = None
                for oc in range(8):
                    if ev_q is not None:
                        b.wait("tensor", sSA, ev_q)
                    for c in range(8):
                        b.op("tensor", lambda t, oc=oc, c=c: t.matmul(
                            pq0[:, :], qw_s[:, c * D + oc * 128: c * D + oc * 128 + 128],
                            lnf2[:, c * 512:(c + 1) * 512], start=(c == 0), stop=False))
                    tq = b.op("tensor", lambda t, oc=oc: t.matmul(
                        pq0[:, :], c_qbr[0:1, oc * 128:(oc + 1) * 128], c_or[:],
                        start=False, stop=True), sPE)
                    b.wait("scalar", sPE, tq)
                    ev_q = b.op("scalar", lambda s, oc=oc, i4=i4: s.activation(
                        qwin2[:, oc * 512:(oc + 1) * 512], pq0[:, :],
                        AF.Copy, scale=CINV8), sSA)
                for oc in range(2):
                    if ev_k is not None:
                        b.wait("tensor", sSA, ev_k)
                    for c in range(8):
                        b.op("tensor", lambda t, oc=oc, c=c: t.matmul(
                            pq1[:, :], kw_s[:, c * 256 + oc * 128: c * 256 + oc * 128 + 128],
                            lnf2[:, c * 512:(c + 1) * 512], start=(c == 0), stop=False))
                    tk = b.op("tensor", lambda t, oc=oc: t.matmul(
                        pq1[:, :], c_kbr[0:1, oc * 128:(oc + 1) * 128], c_or[:],
                        start=False, stop=True), sPE)
                    b.wait("scalar", sPE, tk)
                    b.op("scalar", lambda s, oc=oc, i4=i4: s.activation(
                        k2f[0:64, (2 * oc) * S + i4 * 512: (2 * oc) * S + i4 * 512 + 512],
                        pq1[0:64, :], AF.Copy), sSA)
                    ev_k = b.op("scalar", lambda s, oc=oc, i4=i4: s.activation(
                        k2f[64:128, (2 * oc + 1) * S + i4 * 512: (2 * oc + 1) * S + i4 * 512 + 512],
                        pq1[64:128, :], AF.Copy), sSA)
                b.wait("sync", sSA, ev_q)
                for oc in range(8):
                    b.dma("sync", qsp[oc, :, i4 * 512:(i4 + 1) * 512],
                          qwin2[:, oc * 512:(oc + 1) * 512], sQS2)
                qspill2_mark = sQS2.v
                ev_v = None
                for sc in range(4):
                    st_idx = i4 * 4 + sc
                    if ev_v is not None:
                        b.wait("tensor", sVA, ev_v)
                    for c in range(8):
                        b.op("tensor", lambda t, sc=sc, c=c: t.matmul(
                            pq2[:, :], lnf2[:, c * 512 + sc * 128: c * 512 + sc * 128 + 128],
                            vw_s[:, c * 256:(c + 1) * 256], start=(c == 0), stop=False))
                    tv = b.op("tensor", lambda t, sc=sc: t.matmul(
                        pq2[:, :], c_or[0:1, 0:128], c_vbr[:], start=False, stop=True), sPE)
                    b.wait("vector", sPE, tv)
                    b.wait("vector", sVA, vones)
                    for g in range(4):
                        ev_v = b.op("vector", lambda v, st_idx=st_idx, g=g: v.tensor_copy(
                            v2t[:, st_idx * 260 + g * 65: st_idx * 260 + g * 65 + 64],
                            pq2[:, g * 64:(g + 1) * 64]), sVA)
        QKV_SA = sSA.v
        QKV_VA = sVA.v
        pB1.close()

        pB4 = ExitStack()
        qblk = b.sb("qblk", [128, 2 * 512], BF16, pB4)
        ptb0 = b.sb("ptb0", [128, 1536], BF16, pB4)
        ptb1 = b.sb("ptb1", [128, 1536], BF16, pB4)
        pvsb = b.sb("pvsb", [128, 2 * 512], F32, pB4)
        rlb = b.sb("rlb", [128, 1], F32, pB4)
        psA = b.ps("psA", [128, 1536], F32, pB4)
        psB = b.ps("psB", [128, 1024], F32, pB4)
        ppv2a = b.ps("ppv2a", [128, 512], F32, pB4)
        ppv2b = b.ps("ppv2b", [128, 512], F32, pB4)
        pvt = b.ps("pvt", [128, 260], F32, pB4)
        b.wait("tensor", sSA, QKV_SA)
        b.wait("tensor", sVA, QKV_VA)
        b.wait("sync", sQS2, sQS2.v)
        sQL = (b.sem("sQL0"), b.sem("sQL1"))
        qrel = {}
        qld_t = {}
        it = 0
        ITERS = [(h, qb) for h in range(NH) for qb in range(S // 512)]
        _h0, _qb0 = ITERS[0]
        qld_t[0] = b.dma("sync", qblk[:, 0:512],
                         qsp[HEAD_SLOT[_h0][0], :, _qb0 * 512:(_qb0 + 1) * 512], sQL[0])
        bankB = {0: 0, 1: 0}
        ptbB = {0: 0, 1: 0}
        pv_copy = {0: 0, 1: 0}
        pvt_free = 0
        pend_tail = [None]
        NSK = S // 128
        for h in range(NH):
            jslot, p = HEAD_SLOT[h]
            g = h // 4
            for qb in range(S // 512):
                qsl = it % 2
                if it + 1 < len(ITERS):
                    nh, nqb = ITERS[it + 1]
                    nsl = (it + 1) % 2
                    if it >= 1:
                        b.wait("sync", sPE, qrel[it - 1])
                    qld_t[it + 1] = b.dma("sync", qblk[:, nsl * 512:(nsl + 1) * 512],
                                          qsp[HEAD_SLOT[nh][0], :, nqb * 512:(nqb + 1) * 512],
                                          sQL[nsl])
                b.wait("tensor", sQL[qsl], qld_t[it])
                groups = []
                rem = NSK
                while rem > 0:
                    gsz = 3 if len(groups) % 2 == 0 else 2
                    gsz = min(gsz, rem)
                    groups.append(gsz)
                    rem -= gsz
                starts = []
                acc = 0
                for gw in groups:
                    starts.append(acc)
                    acc += gw
                exp_marks = {}

                def emit_scores(g2, p=p, jslot=jslot, qb=qb, qsl=qsl, g=g):
                    gw = groups[g2]
                    bk = g2 % 2
                    psc = (psA, psB)[bk]
                    b.wait("tensor", sSA, bankB[bk])
                    if g2 == 0:
                        b.wait("tensor", sVA, pv_copy[it % 2])
                    for cc in range(gw):
                        skt = starts[g2] + cc
                        tmm = b.op("tensor", lambda t, cc=cc, skt=skt, psc=psc, g=g,
                                   p=p, jslot=jslot, qb=qb, qsl=qsl: t.matmul(
                            psc[:, cc * 512:(cc + 1) * 512],
                            k2f[:, g * S + skt * 128: g * S + skt * 128 + 128],
                            qblk[:, qsl * 512:(qsl + 1) * 512],
                            start=True, stop=True), sPE if cc == gw - 1 else None)
                    ptb = (ptb0, ptb1)[bk]
                    b.wait("scalar", sPE, tmm)
                    b.wait("scalar", sPE, ptbB[bk])
                    gw_ = gw
                    eexp = b.op("scalar", lambda s, psc=psc, ptb=ptb, gw_=gw_: s.activation(
                        ptb[:, 0:gw_ * 512], psc[:, 0:gw_ * 512], AF.Exp, bias=CNEGC), sSA)
                    bankB[bk] = eexp
                    exp_marks[g2] = eexp

                ppv_it = (ppv2a, ppv2b)[it % 2]

                def emit_pv4(g2, g=g, ppv_it=ppv_it):
                    gw = groups[g2]
                    bk = g2 % 2
                    ptb = (ptb0, ptb1)[bk]
                    b.wait("tensor", sSA, exp_marks[g2])
                    for cc in range(gw):
                        skt = starts[g2] + cc
                        pvm = b.op("tensor", lambda t, cc=cc, skt=skt, ptb=ptb, g=g,
                                   ppv_it=ppv_it: t.matmul(
                            ppv_it[0:65, :], v2t[:, skt * 260 + g * 65: skt * 260 + g * 65 + 65],
                            ptb[:, cc * 512:(cc + 1) * 512],
                            start=(skt == 0), stop=(skt == NSK - 1)),
                            sPE if cc == gw - 1 else None)
                    ptbB[bk] = pvm
                    return pvm

                for g2 in range(len(groups)):
                    emit_scores(g2)
                    if g2 >= 1:
                        pvm = emit_pv4(g2 - 1)
                pvm = emit_pv4(len(groups) - 1)
                qrel[it] = sPE.v
                it += 1

                def emit_tail(ent):
                    tit, th, tqb, tpvm = ent
                    tjslot, tp = HEAD_SLOT[th]
                    ppv = (ppv2a, ppv2b)[tit % 2]
                    psl = tit % 2
                    b.wait("vector", sPE, tpvm)
                    pc = b.op("vector", lambda v, ppv=ppv, psl=psl: v.tensor_copy(
                        pvsb[0:65, psl * 512:(psl + 1) * 512], ppv[0:65, :]), sVA)
                    pv_copy[psl] = pc
                    b.wait("tensor", sVA, pc)
                    b.wait("tensor", sVA, pvt_free)
                    for cc in range(4):
                        tvt = b.op("tensor", lambda t, cc=cc, psl=psl: t.transpose(
                            pvt[:, cc * 65:(cc + 1) * 65],
                            pvsb[0:65, psl * 512 + cc * 128: psl * 512 + (cc + 1) * 128],
                            c_idf[0:65, 0:65]), sPE)
                    b.wait("vector", sPE, tvt)
                    out_free_t = 0
                    for cc in range(4):
                        st_idx = tqb * 4 + cc
                        b.wait("vector", sVA)
                        b.op("vector", lambda v, cc=cc: v.reciprocal(
                            rlb[:], pvt[:, cc * 65 + 64: cc * 65 + 65]), sVA)
                        b.wait("vector", sVA)
                        out_free_t = b.op("vector", lambda v, cc=cc, st_idx=st_idx,
                                          tjslot=tjslot, tp=tp: v.tensor_scalar_mul(
                            attc[:, st_idx * D + tjslot * 128 + tp: st_idx * D + tjslot * 128 + tp + 64],
                            pvt[:, cc * 65: cc * 65 + 64], rlb[:]), sVA)
                    return out_free_t

                if pend_tail[0] is not None:
                    pvt_free = emit_tail(pend_tail[0])
                pend_tail[0] = (it - 1, h, qb, pvm)
        if pend_tail[0] is not None:
            pvt_free = emit_tail(pend_tail[0])
        B4_VA = sVA.v
        if "BX" in phases:
            devb = ExitStack()
            sDVB = b.sem("sDVB")
            b.wait("gpsimd", sVA, B4_VA)
            for i in range(ST):
                st_ = b.dma("gpsimd", out_ap[i * 128:(i + 1) * 128, :],
                            attc[:, i * D:(i + 1) * D], sDVB)
            b.wait("gpsimd", sDVB, sDVB.v)
            devb.close()
            pB4.close()
            pQKV.close()
            pATT.close()
            phB.close()
            with nc.Block() as block:
                @block.sync
                def _(e):
                    for f in b.ops["sync"]:
                        f(e)

                @block.gpsimd
                def _(e):
                    for f in b.ops["gpsimd"]:
                        f(e)

                @block.vector
                def _(e):
                    for f in b.ops["vector"]:
                        f(e)

                @block.scalar
                def _(e):
                    for f in b.ops["scalar"]:
                        f(e)

                @block.tensor
                def _(e):
                    for f in b.ops["tensor"]:
                        f(e)
            ctx.close()
            return nc
        pB4.close()
        pQKV.close()

        pSRC2 = ExitStack()
        src2bf = b.sb("src2bf", [128, ST * D], BF16, pSRC2)
        pB5 = ExitStack()
        ow_s = b.sb("ow_s", [128, 8 * D], BF16, pB5)
        attf = b.sb("attf", [128, 8 * 128], BF16, pB5)
        x1b2 = b.sb("x1b2", [128, 2 * D], F32, pB5)
        ln2t = b.sb("ln2t", [128, D], F32, pB5)
        src1f = b.sb("src1f", [128, D], F32, pB5)
        src2f = b.sb("src2f", [128, 2 * D], F32, pB5)
        s2T = b.sb("s2T", [128, 8 * 128], F32, pB5)
        sqb5 = b.sb("sqb5", [128, D], BF16, pB5)
        st5 = b.sb("st5", [128, 4], F32, pB5)
        pta5 = b.ps("pta5", [128, 128], BF16, pB5)
        ptb5 = b.ps("ptb5", [128, 128], BF16, pB5)
        pgq0 = b.ps("pgq0", [128, 512], F32, pB5)
        pgq1 = b.ps("pgq1", [128, 512], F32, pB5)
        ptf0 = b.ps("ptf0", [128, 128], F32, pB5)
        ptf1 = b.ps("ptf1", [128, 128], F32, pB5)
        pgl = b.ps("pgl", [128, 8], F32, pB5)
        sLB5 = b.sem("sLB5")
        for _e in ("gpsimd", "sync"):
            b.wait(_e, sVA, B4_VA)
            b.wait(_e, sPE, sPE.v)
            b.wait(_e, sSA, sSA.v)
        for c in range(8):
            b.dma("gpsimd", ow_s[:, c * D:(c + 1) * D], ow[c * 128:(c + 1) * 128, :], sLB5)
        for e in ("tensor",):
            b.wait(e, sLB5, sLB5.v)
        b.wait("tensor", sVA, B4_VA)
        sDXd = (b.sem("sDXd0"), b.sem("sDXd1"))
        sS2 = (b.sem("sS20"), b.sem("sS21"))
        tr_ev5 = []
        s2_free = {}
        tf_mark = {0: 0, 1: 0}
        for i in range(ST):
            slot = i % 2
            if i >= 2:
                b.wait("sync", sVA, s2_free[i - 2])
            xld5 = b.dma("sync", x1b2[:, slot * D:(slot + 1) * D],
                         x1h[i * 128:(i + 1) * 128, :], sDXd[slot])
            for c in range(8):
                pt = (pta5, ptb5)[c % 2]
                if len(tr_ev5) >= 2:
                    b.wait("tensor", sVA, tr_ev5[-2])
                tp = b.op("tensor", lambda t, i=i, c=c, pt=pt: t.transpose(
                    pt[:, :], attc[:, i * D + c * 128: i * D + c * 128 + 128], c_idb[:]), sPE)
                b.wait("vector", sPE, tp)
                tr_ev5.append(b.op("vector", lambda v, c=c, pt=pt: v.tensor_copy(
                    attf[:, c * 128:(c + 1) * 128], pt[:, :]), sVA))
            b.wait("tensor", sVA, tr_ev5[-1])
            for half, pgq in ((0, pgq0), (1, pgq1)):
                for c in range(8):
                    b.op("tensor", lambda t, half=half, c=c, pgq=pgq: t.matmul(
                        pgq[:, :], attf[:, c * 128:(c + 1) * 128],
                        ow_s[:, c * D + half * 512: c * D + half * 512 + 512],
                        start=(c == 0), stop=False))
                tow = b.op("tensor", lambda t, half=half, pgq=pgq: t.matmul(
                    pgq[:, :], c_or[0:1, 0:128], c_obr[0:1, half * 512: half * 512 + 512],
                    start=False, stop=True), sPE)
            b.wait("vector", sDXd[slot], xld5)
            b.op("vector", lambda v, i=i, slot=slot: v.tensor_scalar(
                ln2t[:], x1b2[:, slot * D:(slot + 1) * D], m2s[:, i:i + 1], None,
                op0=ALU.subtract), sVA)
            b.wait("vector", sVA)
            b.op("vector", lambda v, i=i: v.tensor_scalar_mul(ln2t[:], ln2t[:], r2s[:, i:i + 1]), sVA)
            b.wait("vector", sVA)
            b.wait("vector", sPE, tow)
            for half, pgq in ((0, pgq0), (1, pgq1)):
                b.op("vector", lambda v, half=half, pgq=pgq: v.tensor_add(
                    src1f[:, half * 512:(half + 1) * 512], pgq[:, :],
                    ln2t[:, half * 512:(half + 1) * 512]), sVA)
            src1_t = sVA.v
            b.wait("scalar", sVA, src1_t)
            b.op("scalar", lambda s: s.activation(sqb5[:], src1f[:], AF.Square,
                                                  accum_out=st5[:, 0:1]), sSA)
            b.wait("scalar", sSA)
            b.op("scalar", lambda s: s.activation(st5[:, 1:2], st5[:, 0:1], AF.Ln,
                                                  scale=CINVD, bias=CEPS6), sSA)
            b.wait("scalar", sSA)
            b.op("scalar", lambda s: s.activation(st5[:, 2:3], st5[:, 1:2], AF.Exp,
                                                  scale=CNEGH), sSA)
            r3_t = sSA.v
            b.wait("vector", sSA, r3_t)
            b.wait("vector", sPE, tf_mark[slot])
            b.wait("vector", sS2[slot], sS2[slot].v)
            s2t = b.op("vector", lambda v, slot=slot: v.scalar_tensor_tensor(
                src2f[:, slot * D:(slot + 1) * D], src1f[:], st5[:, 2:3], c_n1[:],
                op0=ALU.mult, op1=ALU.mult), sVA)
            b.wait("vector", sVA, s2t)
            s2_free[i] = b.op("vector", lambda v, i=i, slot=slot: v.tensor_copy(
                src2bf[:, i * D:(i + 1) * D], src2f[:, slot * D:(slot + 1) * D]), sVA)
            b.wait("sync", sVA, s2_free[i])
            b.dma("sync", s2h[i * 128:(i + 1) * 128, :], src2bf[:, i * D:(i + 1) * D], sS2[slot])
            b.wait("tensor", sVA, s2_free[i])
            for c in range(8):
                ptf = (ptf0, ptf1)[c % 2]
                if c >= 2:
                    b.wait("tensor", sSA)
                tpf = b.op("tensor", lambda t, c=c, slot=slot, ptf=ptf: t.transpose(
                    ptf[:, :], src2f[:, slot * D + c * 128: slot * D + c * 128 + 128],
                    c_idf[:]), sPE)
                b.wait("scalar", sPE, tpf)
                b.op("scalar", lambda s, c=c, ptf=ptf: s.activation(
                    s2T[:, c * 128:(c + 1) * 128], ptf[:, :], AF.Copy), sSA)
            tf_mark[slot] = sPE.v
            b.wait("tensor", sSA)
            for c in range(8):
                tgl = b.op("tensor", lambda t, c=c: t.matmul(
                    pgl[:, :], s2T[:, c * 128:(c + 1) * 128], c_gw[:, c * 8:(c + 1) * 8],
                    start=(c == 0), stop=(c == 7)), sPE if c == 7 else None)
            b.wait("vector", sPE, tgl)
            gl_ev = b.op("vector", lambda v, i=i: v.tensor_copy(
                logits[:, i * 8:(i + 1) * 8], pgl[:, :]), sVA)
            b.wait("tensor", sVA, gl_ev)
        B5_VA = sVA.v
        B5_SS = (sS2[0], sS2[0].v, sS2[1], sS2[1].v)
        pB5.close()

        if "src2" in tap_map:
            b.wait("sync", sS2[0], B5_SS[1])
            b.wait("sync", sS2[1], B5_SS[3])
            tb2 = b.sb("tapb2", [128, D], BF16)
            sTPg = b.sem("sTPg")
            for i in range(ST):
                t = b.dma("gpsimd", tb2[:], s2h[i * 128:(i + 1) * 128, :], sTPg)
                b.wait("gpsimd", sTPg, t)
                t2 = b.dma("gpsimd", tap_map["src2"][i * 128:(i + 1) * 128, :], tb2[:], sTPg)
                b.wait("gpsimd", sTPg, t2)

        pRT = ExitStack()
        E8 = ST * 8
        m8s = b.sb("m8s", [128, 8], F32, pRT)
        i8s = b.sb("i8s", [128, 8], U32, pRT)
        idx0f = b.sb("idx0f", [128, ST], F32, pRT)
        idx1f = b.sb("idx1f", [128, ST], F32, pRT)
        dcol = b.sb("dcol", [128, 1], F32, pRT)
        ind0 = b.sb("ind0", [128, E8], F32, pRT)
        ind1 = b.sb("ind1", [128, E8], F32, pRT)
        indsum = b.sb("indsum", [128, E8], F32, pRT)
        idsb = b.sb("idsb", [128, E8], BF16, pRT)
        Ssb = b.sb("Ssb", [128, E8], F32, pRT)
        Tsb = b.sb("Tsb", [1, E8], F32, pRT)
        crow0 = b.sb("crow0", [1, E8], F32, pRT)
        crow1 = b.sb("crow1", [1, E8], F32, pRT)
        cntrow = b.sb("cntrow", [1, 8], F32, pRT)
        aggsb = b.sb("aggsb", [NCORES, 8], F32, pRT)
        offrow = b.sb("offrow", [1, 8], F32, pRT)
        bcrow = b.sb("bcrow", [1, 2 * E8 if 2 * E8 >= 2 * NBLK else 2 * NBLK], F32, pRT)
        zrow = b.sb("zrow", [1, ST], F32, pRT)
        cbb = b.sb("cbb", [128, 2 * E8 if 2 * E8 >= 2 * NBLK else 2 * NBLK], F32, pRT)
        gincl = b.sb("gincl", [128, E8], F32, pRT)
        rank0 = b.sb("rank0", [128, E8], F32, pRT)
        rank1 = b.sb("rank1", [128, E8], F32, pRT)
        keep0 = b.sb("keep0", [128, E8], F32, pRT)
        keep1 = b.sb("keep1", [128, E8], F32, pRT)
        ksum = b.sb("ksum", [128, E8], BF16, pRT)
        vcrow = b.sb("vcrow", [1, 8], F32, pRT)
        nbtmp = b.sb("nbtmp", [1, 16], F32, pRT)
        nbrow = b.sb("nbrow", [1, 8], F32, pRT)
        bsrow = b.sb("bsrow", [1, 8], F32, pRT)
        berow = b.sb("berow", [1, 8], F32, pRT)
        barow = b.sb("barow", [1, 8], F32, pRT)
        eblk = b.sb("eblk", [1, NBLK], F32, pRT)
        ebt = b.sb("ebt", [1, NBLK], F32, pRT)
        sel0 = b.sb("sel0", [128, ST], F32, pRT)
        sel1 = b.sb("sel1", [128, ST], F32, pRT)
        ksel0 = b.sb("ksel0", [128, ST], F32, pRT)
        ksel1 = b.sb("ksel1", [128, ST], F32, pRT)
        poss0 = b.sb("poss0", [128, ST], F32, pRT)
        poss1 = b.sb("poss1", [128, ST], F32, pRT)
        psS = b.ps("psS", [128, E8], F32, pRT)
        psT = b.ps("psT", [1, E8], F32, pRT)
        psO = b.ps("psO", [1, 8], F32, pRT)
        psBC = b.ps("psBC", [128, 2 * E8 if 2 * E8 >= 2 * NBLK else 2 * NBLK], F32, pRT)

        b.wait("scalar", sVA, B5_VA)
        b.wait("vector", sLB5, sLB5.v)
        b.wait("scalar", sLB5, sLB5.v)
        b.op("scalar", lambda s: s.activation(probs[:], logits[:], AF.Sigmoid), sSA)
        PRB = sSA.v
        if "probs" in tap_map:
            b.wait("sync", sSA, PRB)
            b.dma("sync", tap_map["probs"], probs[:], b.sem("sTPP"))
        b.wait("vector", sSA, PRB)
        for f in range(ST):
            b.op("vector", lambda v, f=f: v.max(out=m8s[:], in_=probs[:, f * 8:(f + 1) * 8]), sVA)
            b.wait("vector", sVA)
            b.op("vector", lambda v, f=f: v.max_index(out=i8s[:], in_max=m8s[:],
                                                 in_values=probs[:, f * 8:(f + 1) * 8]), sVA)
            b.wait("vector", sVA)
            b.op("vector", lambda v, f=f: v.tensor_copy(idx0f[:, f:f + 1], i8s[:, 0:1].bitcast(I32)), sVA)
            b.op("vector", lambda v, f=f: v.tensor_copy(idx1f[:, f:f + 1], i8s[:, 1:2].bitcast(I32)), sVA)
            b.op("vector", lambda v: v.tensor_add(dcol[:], m8s[:, 0:1], m8s[:, 1:2]), sVA)
            b.wait("vector", sVA)
            b.op("vector", lambda v: v.tensor_scalar_add(dcol[:], dcol[:], 1e-6), sVA)
            b.wait("vector", sVA)
            b.op("vector", lambda v: v.reciprocal(dcol[:], dcol[:]), sVA)
            b.wait("vector", sVA)
            b.op("vector", lambda v, f=f: v.tensor_scalar_mul(w0c[:, f:f + 1], m8s[:, 0:1], dcol[:]), sVA)
            b.op("vector", lambda v, f=f: v.tensor_scalar_mul(w1c[:, f:f + 1], m8s[:, 1:2], dcol[:]), sVA)
            b.wait("vector", sVA)
        for e in range(NEXP):
            b.op("vector", lambda v, e=e: v.tensor_scalar(
                bass.AP(ind0.ap().tensor, e, [[E8, 128], [8, ST]]), idx0f[:],
                float(e), None, op0=ALU.is_equal), sVA)
            b.op("vector", lambda v, e=e: v.tensor_scalar(
                bass.AP(ind1.ap().tensor, e, [[E8, 128], [8, ST]]), idx1f[:],
                float(e), None, op0=ALU.is_equal), sVA)
        b.wait("vector", sVA)
        b.op("vector", lambda v: v.tensor_add(indsum[:], ind0[:], ind1[:]), sVA)
        b.wait("vector", sVA)
        ids_t = b.op("vector", lambda v: v.tensor_copy(idsb[:], indsum[:]), sVA)
        b.wait("tensor", sVA, ids_t)
        tS = b.op("tensor", lambda t: t.matmul(psS[:], c_triu[:], idsb[:], start=True, stop=True), sPE)
        tT = b.op("tensor", lambda t: t.matmul(psT[:], c_oc[:], idsb[:], start=True, stop=True), sPE)
        b.wait("vector", sPE, tT)
        b.op("vector", lambda v: v.tensor_copy(Ssb[:], psS[:]), sVA)
        b.op("vector", lambda v: v.tensor_copy(Tsb[:], psT[:]), sVA)
        b.wait("vector", sVA)
        b.op("vector", lambda v: v.memset(crow0[:], 0.0), sVA)
        b.wait("vector", sVA)
        b.op("vector", lambda v: v.tensor_copy(crow0[0:1, 8:E8], Tsb[0:1, 0:E8 - 8]), sVA)
        b.wait("vector", sVA)
        cur, nxt = crow0, crow1
        sh = 8
        while sh < E8:
            b.op("vector", lambda v, cur=cur, nxt=nxt, sh=sh: v.tensor_copy(
                nxt[0:1, 0:sh], cur[0:1, 0:sh]), sVA)
            b.op("vector", lambda v, cur=cur, nxt=nxt, sh=sh: v.tensor_add(
                nxt[0:1, sh:E8], cur[0:1, sh:E8], cur[0:1, 0:E8 - sh]), sVA)
            b.wait("vector", sVA)
            cur, nxt = nxt, cur
            sh *= 2
        carry = cur
        for e in range(NEXP):
            b.op("vector", lambda v, e=e: v.reduce_sum(
                cntrow[0:1, e:e + 1], bass.AP(Tsb.ap().tensor, e, [[E8, 1], [8, ST]]),
                axis=AX.X), sVA)
        CNT_VA = sVA.v
        sAGD = b.sem("sAGD")
        b.wait("gpsimd", sVA, CNT_VA)
        b.dma("gpsimd", ag_in, cntrow[:], sAGD)
        b.wait("gpsimd", sAGD, sAGD.v)
        b.op("gpsimd", lambda g: g.collective_compute(
            "AllGather", ALU.bypass, replica_groups=[list(range(NCORES))],
            ins=[ag_in], outs=[ag_out]), sCC)
        b.wait("gpsimd", sCC, sCC.v)
        agld = b.dma("gpsimd", aggsb[:], ag_out, sAGD)
        b.wait("tensor", sAGD, agld)
        tO = b.op("tensor", lambda t: t.matmul(psO[:], c_cm[:], aggsb[:], start=True, stop=True), sPE)
        b.wait("vector", sPE, tO)
        b.op("vector", lambda v: v.tensor_copy(offrow[:], psO[:]), sVA)
        b.wait("vector", sVA)
        for e in range(NEXP):
            b.op("vector", lambda v, e=e, carry=carry: v.tensor_scalar_add(
                bass.AP(bcrow.ap().tensor, e, [[bcrow.shape[1], 1], [8, ST]]),
                bass.AP(carry.ap().tensor, e, [[256, 1], [8, ST]]),
                offrow[0:1, e:e + 1]), sVA)
        b.op("vector", lambda v: v.memset(zrow[:], 0.0), sVA)
        BC1 = sVA.v
        b.wait("vector", sVA, BC1)
        bc_t = sVA.v
        b.wait("tensor", sVA, BC1)
        tBC = b.op("tensor", lambda t: t.matmul(psBC[:, 0:E8], c_orf[:], bcrow[0:1, 0:E8], start=True, stop=True), sPE)
        b.wait("vector", sPE, tBC)
        b.op("vector", lambda v: v.tensor_copy(cbb[:, 0:E8], psBC[:, 0:E8]), sVA)
        b.wait("vector", sVA)
        b.op("vector", lambda v: v.tensor_add(gincl[:], Ssb[:], cbb[:, 0:E8]), sVA)
        b.wait("vector", sVA)
        b.op("vector", lambda v: v.tensor_sub(rank0[:], gincl[:], indsum[:]), sVA)
        b.op("vector", lambda v: v.tensor_sub(rank1[:], gincl[:], ind1[:]), sVA)
        b.wait("vector", sVA)
        b.op("vector", lambda v: v.tensor_scalar(keep0[:], rank0[:], float(CAP - 1), None,
                                                 op0=ALU.is_le), sVA)
        b.op("vector", lambda v: v.tensor_scalar(keep1[:], rank1[:], float(CAP - 1), None,
                                                 op0=ALU.is_le), sVA)
        b.wait("vector", sVA)
        b.op("vector", lambda v: v.tensor_mul(keep0[:], keep0[:], ind0[:]), sVA)
        b.op("vector", lambda v: v.tensor_mul(keep1[:], keep1[:], ind1[:]), sVA)
        b.wait("vector", sVA)
        b.op("vector", lambda v: v.tensor_add(ksum[:], keep0[:], keep1[:]), sVA)
        KS = sVA.v
        b.wait("tensor", sVA, KS)
        tT2 = b.op("tensor", lambda t: t.matmul(psT[:], c_oc[:], ksum[:], start=True, stop=True), sPE)
        b.wait("vector", sPE, tT2)
        b.op("vector", lambda v: v.tensor_copy(Tsb[:], psT[:]), sVA)
        b.wait("vector", sVA)
        for e in range(NEXP):
            b.op("vector", lambda v, e=e: v.reduce_sum(
                vcrow[0:1, e:e + 1], bass.AP(Tsb.ap().tensor, e, [[E8, 1], [8, ST]]),
                axis=AX.X), sVA)
        b.wait("vector", sVA)
        for e in range(NEXP):
            b.op("vector", lambda v, e=e: v.tensor_scalar(
                nbtmp[:], c_thr[:], vcrow[0:1, e:e + 1], None, op0=ALU.is_lt), sVA)
            b.wait("vector", sVA)
            b.op("vector", lambda v, e=e: v.reduce_sum(nbrow[0:1, e:e + 1], nbtmp[:], axis=AX.X), sVA)
            b.wait("vector", sVA)
        b.op("vector", lambda v, bs=bsrow: v.tensor_copy(bs[:], nbrow[:]), sVA)
        b.wait("vector", sVA)
        for sh in (1, 2, 4):
            b.op("vector", lambda v, sh=sh, bs=bsrow, be=berow: v.tensor_copy(
                be[0:1, 0:sh], bs[0:1, 0:sh]), sVA)
            b.op("vector", lambda v, sh=sh, bs=bsrow, be=berow: v.tensor_add(
                be[0:1, sh:8], bs[0:1, sh:8], bs[0:1, 0:8 - sh]), sVA)
            b.wait("vector", sVA)
            bsrow, berow = berow, bsrow
        b.op("vector", lambda v, bs=bsrow: v.tensor_sub(bs[:], bs[:], nbrow[:]), sVA)
        b.wait("vector", sVA)
        b.op("vector", lambda v, bs=bsrow, be=berow: v.tensor_add(be[:], bs[:], nbrow[:]), sVA)
        b.wait("vector", sVA)
        b.op("vector", lambda v, bs=bsrow: v.tensor_scalar_mul(barow[:], bs[:], float(CBLK)), sVA)
        b.wait("vector", sVA)
        b.op("vector", lambda v: v.tensor_sub(barow[:], barow[:], offrow[:]), sVA)
        b.wait("vector", sVA)
        for e in range(NEXP):
            b.op("vector", lambda v, e=e: v.tensor_scalar_add(
                bass.AP(bcrow.ap().tensor, E8 + e, [[bcrow.shape[1], 1], [8, ST]]),
                zrow[0:1, :], barow[0:1, e:e + 1]), sVA)
        BC2 = sVA.v
        b.wait("tensor", sVA, BC2)
        tBC2 = b.op("tensor", lambda t: t.matmul(psBC[:, E8:2 * E8], c_orf[:], bcrow[0:1, E8:2 * E8], start=True, stop=True), sPE)
        b.wait("vector", sPE, tBC2)
        b.op("vector", lambda v: v.tensor_copy(cbb[:, E8:2 * E8], psBC[:, E8:2 * E8]), sVA)
        b.wait("vector", sVA)
        b.op("vector", lambda v: v.memset(eblk[:], 0.0), sVA)
        b.wait("vector", sVA)
        for e in range(1, NEXP):
            b.op("vector", lambda v, e=e, bs=bsrow: v.tensor_scalar(
                ebt[:], c_js[:], bs[0:1, e:e + 1], None, op0=ALU.is_ge), sVA)
            b.op("vector", lambda v, e=e, be=berow: v.tensor_scalar(
                wofff[:], c_js[:], be[0:1, e:e + 1], None, op0=ALU.is_lt), sVA)
            b.wait("vector", sVA)
            b.op("vector", lambda v: v.tensor_mul(ebt[:], ebt[:], wofff[:]), sVA)
            b.wait("vector", sVA)
            b.op("vector", lambda v, e=e: v.tensor_scalar(
                ebt[:], ebt[:], float(e), None, op0=ALU.mult), sVA)
            b.wait("vector", sVA)
            b.op("vector", lambda v: v.tensor_add(eblk[:], eblk[:], ebt[:]), sVA)
            b.wait("vector", sVA)
        b.op("vector", lambda v: v.tensor_scalar(
            wobase[0:1, 0:NBLK], eblk[:], 128.0, None, op0=ALU.mult), sVA)
        WOB = sVA.v
        b.wait("tensor", sVA, WOB)
        tWB = b.op("tensor", lambda t: t.matmul(psBC[:, 0:NBLK], c_orf[:],
                                                wobase[0:1, 0:NBLK], start=True, stop=True), sPE)
        b.wait("vector", sPE, tWB)
        b.op("vector", lambda v: v.tensor_scalar(
            widxf[:, 0:NBLK], psBC[:, 0:NBLK], c_iota[:], None, op0=ALU.add), sVA)
        b.wait("vector", sVA)
        woff_t = b.op("vector", lambda v: v.tensor_copy(widxu[:, 0:NBLK], widxf[:, 0:NBLK]), sVA)
        b.op("vector", lambda v: v.tensor_add(rank0[:], rank0[:], cbb[:, E8:2 * E8]), sVA)
        b.op("vector", lambda v: v.tensor_add(rank1[:], rank1[:], cbb[:, E8:2 * E8]), sVA)
        b.wait("vector", sVA)
        b.op("vector", lambda v: v.tensor_mul(rank0[:], rank0[:], ind0[:]), sVA)
        b.op("vector", lambda v: v.tensor_mul(rank1[:], rank1[:], ind1[:]), sVA)
        b.wait("vector", sVA)
        b.op("vector", lambda v: v.tensor_reduce(
            sel0[:], rank0[:].rearrange("p (f e) -> p f e", e=8), axis=AX.X, op=ALU.add), sVA)
        b.op("vector", lambda v: v.tensor_reduce(
            sel1[:], rank1[:].rearrange("p (f e) -> p f e", e=8), axis=AX.X, op=ALU.add), sVA)
        b.op("vector", lambda v: v.tensor_reduce(
            ksel0[:], keep0[:].rearrange("p (f e) -> p f e", e=8), axis=AX.X, op=ALU.add), sVA)
        b.op("vector", lambda v: v.tensor_reduce(
            ksel1[:], keep1[:].rearrange("p (f e) -> p f e", e=8), axis=AX.X, op=ALU.add), sVA)
        b.wait("vector", sVA)
        for sel, ksel, poss, scu in ((sel0, ksel0, poss0, sc0u), (sel1, ksel1, poss1, sc1u)):
            b.op("vector", lambda v, sel=sel, ksel=ksel, poss=poss: v.tensor_mul(
                poss[:], sel[:], ksel[:]), sVA)
            b.op("vector", lambda v, ksel=ksel: v.tensor_scalar(
                idx0f[:], ksel[:], -float(NCROWS), float(NCROWS), op0=ALU.mult, op1=ALU.add), sVA)
            b.wait("vector", sVA)
            b.op("vector", lambda v, poss=poss: v.tensor_add(poss[:], poss[:], idx0f[:]), sVA)
            b.wait("vector", sVA)
            b.op("vector", lambda v, poss=poss, scu=scu: v.tensor_copy(scu[:], poss[:]), sVA)
            b.wait("vector", sVA)
        tFR = b.op("tensor", lambda t: t.matmul(psT[0:128, 0:ST] if False else psS[:, 0:ST],
                                                c_orf[:], c_framp[:], start=True, stop=True), sPE)
        b.wait("vector", sPE, tFR)
        b.op("vector", lambda v: v.tensor_copy(fbb[:], psS[:, 0:ST]), sVA)
        b.wait("vector", sVA)
        b.op("vector", lambda v: v.tensor_scalar(fbb[:], fbb[:], c_iota[:], float(NCROWS),
                                                 op0=ALU.add, op1=ALU.add), sVA)
        b.wait("vector", sVA)
        for sel, ksel, scu, extra in ((sel0, ksel0, ssc0, 0.0), (sel1, ksel1, ssc1, 128.0)):
            b.op("vector", lambda v, sel=sel, extra=extra: v.tensor_scalar(
                idx1f[:], fbb[:], extra, None, op0=ALU.add), sVA)
            b.wait("vector", sVA)
            b.op("vector", lambda v, sel=sel: v.tensor_sub(idx0f[:], sel[:], idx1f[:]), sVA)
            b.wait("vector", sVA)
            b.op("vector", lambda v, ksel=ksel: v.tensor_mul(idx0f[:], idx0f[:], ksel[:]), sVA)
            b.wait("vector", sVA)
            b.op("vector", lambda v: v.tensor_add(idx0f[:], idx0f[:], idx1f[:]), sVA)
            b.wait("vector", sVA)
            b.op("vector", lambda v, scu=scu: v.tensor_copy(scu[:], idx0f[:]), sVA)
            b.wait("vector", sVA)
        ROUTE_VA = sVA.v
        if "route" in tap_map:
            rt = b.sb("rtap", [128, 256], F32, pRT)
            for ci, src_ in enumerate((idx1f, sel0, sel1, ksel0, ksel1, poss0, poss1, w0c)):
                b.op("vector", lambda v, ci=ci, src_=src_: v.tensor_copy(
                    rt[:, ci * 32:(ci + 1) * 32], src_[:]), sVA)
            b.wait("sync", sVA)
            b.dma("sync", tap_map["route"], rt[:], b.sem("sTPR"))
        if "cnt" in tap_map:
            ct = b.sb("ctap", [1, 24], F32, pRT)
            b.op("vector", lambda v: v.tensor_copy(ct[0:1, 0:8], vcrow[:]), sVA)
            b.op("vector", lambda v: v.tensor_copy(ct[0:1, 8:16], nbrow[:]), sVA)
            b.op("vector", lambda v: v.tensor_copy(ct[0:1, 16:24], offrow[:]), sVA)
            b.wait("sync", sVA)
            b.dma("sync", tap_map["cnt"], ct[:], b.sem("sTPC"))
        pRT.close()
        if "C" not in phases:
            pSRC2.close()
            pATT.close()

    if "C" in phases:
        sSCT = b.sem("sSCT")
        b.wait("gpsimd", sGW, GW_TOTAL)
        b.wait("gpsimd", sVA, ROUTE_VA)
        for f in range(ST):
            b.op("gpsimd", lambda g, f=f: g.indirect_dma_start(
                out=cin, out_offset=IndirectOffsetOnAxis(ssc0[:, f:f + 1], 0),
                in_=src2bf[:, f * D:(f + 1) * D], in_offset=None), sSCT, 16)
            b.op("gpsimd", lambda g, f=f: g.indirect_dma_start(
                out=cin, out_offset=IndirectOffsetOnAxis(ssc1[:, f:f + 1], 0),
                in_=src2bf[:, f * D:(f + 1) * D], in_offset=None), sSCT, 16)
        SCT_ALL = sSCT.v
        b.wait("gpsimd", sSCT, SCT_ALL)
        pSRC2.close()
        pATT.close()

        pC2 = ExitStack()
        w1s = [b.sb("w1s0", [128, 8 * DFF], BF16, pC2), b.sb("w1s1", [128, 8 * DFF], BF16, pC2)]
        w2s = [b.sb("w2s0", [128, 16 * D], BF16, pC2), b.sb("w2s1", [128, 16 * D], BF16, pC2)]
        xfm = [b.sb("xfm0", [128, 8 * 512], BF16, pC2),
               b.sb("xfm1", [128, 8 * 512], BF16, pC2)]
        hfm = b.sb("hfm", [128, 16 * 512], BF16, pC2)
        outt = [b.sb("outt0", [128, D], BF16, pC2), b.sb("outt1", [128, D], BF16, pC2)]
        ph0 = b.ps("ph0", [128, 512], F32, pC2)
        ph1 = b.ps("ph1", [128, 512], F32, pC2)
        po0 = b.ps("po0", [128, 512], F32, pC2)
        po1 = b.ps("po1", [128, 512], F32, pC2)
        sWB = (b.sem("sWB0"), b.sem("sWB1"))
        sCI = (b.sem("sCI0"), b.sem("sCI1"))
        sCO = (b.sem("sCO0"), b.sem("sCO1"))
        wfree = {0: 0, 1: 0}
        xfree_c = {0: 0, 1: 0}
        ci_t = {0: 0, 1: 0}
        b.wait("gpsimd", sVA, woff_t)
        b.wait("sync", sSCT, SCT_ALL)
        for _e in ("gpsimd", "sync"):
            b.wait(_e, sS2[0], B5_SS[1])
            b.wait(_e, sS2[1], B5_SS[3])
        hfm_free = 0
        out_free = {0: 0, 1: 0}
        po_free = [0]
        for j in range(NBLK):
            par = j % 2
            b.wait("gpsimd", sPE, wfree[par])
            b.op("gpsimd", lambda g, j=j, par=par: g.indirect_dma_start(
                out=w1s[par][:], out_offset=None,
                in_=w1b, in_offset=IndirectOffsetOnAxis(widxu[:, j:j + 1], 0)), sWB[par], 16)
            b.op("gpsimd", lambda g, j=j, par=par: g.indirect_dma_start(
                out=w2s[par][:], out_offset=None,
                in_=w2b, in_offset=IndirectOffsetOnAxis(widxu[:, j:j + 1], 0)), sWB[par], 16)
            WB_T = sWB[par].v
            for half_b in range(CBLK // 512):
                hpar = (j * (CBLK // 512) + half_b) % 2
                rbase = j * CBLK + half_b * 512
                gi = j * (CBLK // 512) + half_b
                if gi == 0:
                    for c in range(8):
                        cld0 = b.dma("sync", xfm[0][:, c * 512:(c + 1) * 512],
                                     cin[0:512, c * 128:(c + 1) * 128], sCI[0],
                                     transpose=True)
                    ci_t[0] = cld0
                if gi + 1 < NBLK * (CBLK // 512):
                    npar = (gi + 1) % 2
                    nbase = (gi + 1) * 512
                    b.wait("sync", sPE, xfree_c[npar])
                    for c in range(8):
                        cldn = b.dma("sync", xfm[npar][:, c * 512:(c + 1) * 512],
                                     cin[nbase:nbase + 512, c * 128:(c + 1) * 128],
                                     sCI[npar], transpose=True)
                    ci_t[npar] = cldn
                b.wait("tensor", sCI[hpar], ci_t[hpar])
                b.wait("tensor", sWB[par], WB_T)
                b.wait("tensor", sSA)
                b.wait("scalar", sPE, hfm_free)
                hbank = {0: 0, 1: 0}
                for hc in range(16):
                    hb = hc % 2
                    ph = (ph0, ph1)[hb]
                    b.wait("tensor", sSA, hbank[hb])
                    for c in range(8):
                        th = b.op("tensor", lambda t, hc=hc, c=c, par=par, ph=ph,
                                  hpar=hpar: t.matmul(
                            ph[:, :], w1s[par][:, c * DFF + hc * 128: c * DFF + hc * 128 + 128],
                            xfm[hpar][:, c * 512:(c + 1) * 512], start=(c == 0), stop=(c == 7)),
                            sPE if c == 7 else None)
                    b.wait("scalar", sPE, th)
                    hbank[hb] = b.op("scalar", lambda s, hc=hc, ph=ph: s.activation(
                        hfm[:, hc * 512:(hc + 1) * 512], ph[:, :], AF.Gelu), sSA)
                xfree_c[hpar] = sPE.v
                GEL = sSA.v
                b.wait("tensor", sSA, GEL)
                for r in range(4):
                    opar = r % 2
                    b.wait("tensor", sCO[opar], out_free[opar])
                    b.wait("tensor", sVA, po_free[0])
                    for half, po in ((0, po0), (1, po1)):
                        for fc in range(16):
                            tw2 = b.op("tensor", lambda t, r=r, half=half, fc=fc, par=par, po=po: t.matmul(
                                po[:, :], hfm[:, fc * 512 + r * 128: fc * 512 + r * 128 + 128],
                                w2s[par][:, fc * D + half * 512: fc * D + half * 512 + 512],
                                start=(fc == 0), stop=(fc == 15)), sPE if fc == 15 else None)
                    b.wait("vector", sPE, tw2)
                    for half, po in ((0, po0), (1, po1)):
                        oev = b.op("vector", lambda v, half=half, po=po, opar=opar: v.tensor_copy(
                            outt[opar][:, half * 512:(half + 1) * 512], po[:, :]), sVA)
                    po_free[0] = oev
                    b.wait("sync", sVA, oev)
                    b.dma("sync", cout[rbase + r * 128: rbase + (r + 1) * 128, :],
                          outt[opar][:], sCO[opar])
                    out_free[opar] = sCO[opar].v
                hfm_free = sPE.v
            hfm_free = sPE.v
            wfree[par] = sPE.v
        CO_ALL = (sCO[0].v, sCO[1].v)
        pC2.close()

        pC3 = ExitStack()
        g0 = b.sb("g0", [128, 2 * D], BF16, pC3)
        g1 = b.sb("g1", [128, 2 * D], BF16, pC3)
        s2b3 = b.sb("s2b3", [128, 2 * D], BF16, pC3)
        x1b3 = b.sb("x1b3", [128, 2 * D], F32, pC3)
        src3 = b.sb("src3", [128, D], F32, pC3)
        sqb3 = b.sb("sqb3", [128, D], BF16, pC3)
        st6 = b.sb("st6", [128, 4], F32, pC3)
        outf = b.sb("outf", [128, 2 * D], F32, pC3)
        sGA = (b.sem("sGA0"), b.sem("sGA1"))
        sLD3 = (b.sem("sLD30"), b.sem("sLD31"))
        sOUTp = (b.sem("sOUT0"), b.sem("sOUT1"))
        b.wait("gpsimd", sCO[0], CO_ALL[0])
        b.wait("gpsimd", sCO[1], CO_ALL[1])
        b.wait("sync", sCO[0], CO_ALL[0])
        b.wait("sync", sCO[1], CO_ALL[1])
        gfree = {}
        outfree = {}
        for f in range(ST):
            par = f % 2
            mz = b.op("vector", lambda v, par=par: v.memset(g0[:, par * D:(par + 1) * D], 0.0), sVA)
            mz2 = b.op("vector", lambda v, par=par: v.memset(g1[:, par * D:(par + 1) * D], 0.0), sVA)
            b.wait("gpsimd", sVA, mz2)
            b.op("gpsimd", lambda g, f=f, par=par: g.indirect_dma_start(
                out=g0[:, par * D:(par + 1) * D], out_offset=None,
                in_=cout, in_offset=IndirectOffsetOnAxis(sc0u[:, f:f + 1], 0),
                compute_op=ALU.add), sGA[par], 16)
            ga = b.op("gpsimd", lambda g, f=f, par=par: g.indirect_dma_start(
                out=g1[:, par * D:(par + 1) * D], out_offset=None,
                in_=cout, in_offset=IndirectOffsetOnAxis(sc1u[:, f:f + 1], 0),
                compute_op=ALU.add), sGA[par], 16)
            if f >= 2:
                b.wait("sync", sVA, gfree[f - 2])
            l1 = b.dma("sync", s2b3[:, par * D:(par + 1) * D], s2h[f * 128:(f + 1) * 128, :], sLD3[par])
            l2 = b.dma("sync", x1b3[:, par * D:(par + 1) * D], x1h[f * 128:(f + 1) * 128, :], sLD3[par])
            b.wait("vector", sGA[par], ga)
            b.wait("vector", sLD3[par], l2)
            t1 = b.op("vector", lambda v, f=f, par=par: v.scalar_tensor_tensor(
                src3[:], g0[:, par * D:(par + 1) * D], w0c[:, f:f + 1],
                s2b3[:, par * D:(par + 1) * D], op0=ALU.mult, op1=ALU.add), sVA)
            b.wait("vector", sVA, t1)
            t2 = b.op("vector", lambda v, f=f, par=par: v.scalar_tensor_tensor(
                src3[:], g1[:, par * D:(par + 1) * D], w1c[:, f:f + 1],
                src3[:], op0=ALU.mult, op1=ALU.add), sVA)
            gfree[f] = t2
            b.wait("scalar", sVA, t2)
            b.op("scalar", lambda s: s.activation(sqb3[:], src3[:], AF.Square,
                                                  accum_out=st6[:, 0:1]), sSA)
            b.wait("scalar", sSA)
            b.op("scalar", lambda s: s.activation(st6[:, 1:2], st6[:, 0:1], AF.Ln,
                                                  scale=CINVD, bias=CEPS6), sSA)
            b.wait("scalar", sSA)
            b.op("scalar", lambda s: s.activation(st6[:, 2:3], st6[:, 1:2], AF.Exp,
                                                  scale=CNEGH), sSA)
            r6 = sSA.v
            b.wait("vector", sSA, r6)
            if f >= 2:
                b.wait("vector", sOUTp[par], outfree[f - 2])
            t3 = b.op("vector", lambda v, par=par: v.scalar_tensor_tensor(
                outf[:, par * D:(par + 1) * D], src3[:], st6[:, 2:3], c_n2[:],
                op0=ALU.mult, op1=ALU.mult), sVA)
            b.wait("vector", sVA, t3)
            t4 = b.op("vector", lambda v, par=par: v.tensor_add(
                outf[:, par * D:(par + 1) * D], outf[:, par * D:(par + 1) * D],
                x1b3[:, par * D:(par + 1) * D]), sVA)
            b.wait("sync", sVA, t4)
            b.dma("sync", out_ap[f * 128:(f + 1) * 128, :], outf[:, par * D:(par + 1) * D], sOUTp[par])
            outfree[f] = sOUTp[par].v
        b.wait("sync", sOUTp[0], sOUTp[0].v)
        b.wait("sync", sOUTp[1], sOUTp[1].v)
        pC3.close()
    if "B" in phases:
        phB.close()

    if "C" not in phases:
        devp = ExitStack()
        sDV = b.sem("sDV")
        tbuf = b.sb("tbuf", [128, D], F32, devp)
        for i in range(ST):
            t = b.dma("sync", tbuf[:], x1h[i * 128:(i + 1) * 128, :], sDV)
            b.wait("sync", sDV, t)
            st = b.dma("sync", out_ap[i * 128:(i + 1) * 128, :], tbuf[:], sOUT)
            b.wait("sync", sOUT, st)
        devp.close()

    with nc.Block() as block:
        @block.sync
        def _(e):
            for f in b.ops["sync"]:
                f(e)

        @block.gpsimd
        def _(e):
            for f in b.ops["gpsimd"]:
                f(e)

        @block.vector
        def _(e):
            for f in b.ops["vector"]:
                f(e)

        @block.scalar
        def _(e):
            for f in b.ops["scalar"]:
                f(e)

        @block.tensor
        def _(e):
            for f in b.ops["tensor"]:
                f(e)

    ctx.close()
    return nc


def make_aux(core_id):
    am = np.zeros((128, 128), np.float32)
    am[np.arange(128)[:, None] < np.arange(128)[None, :]] = -1e9
    cm = np.zeros((NCORES, 1), np.float32)
    cm[:core_id, 0] = 1.0
    return {
        "identf": np.eye(128, dtype=np.float32),
        "identb": np.eye(128, dtype=np.float32).astype(bfdt),
        "triub": np.triu(np.ones((128, 128), np.float32)).astype(bfdt),
        "amask": am,
        "onesrow": np.ones((1, 512), np.float32).astype(bfdt),
        "onesrow_f": np.ones((1, 128), np.float32),
        "onescol": np.ones((128, 1), np.float32).astype(bfdt),
        "coremask": cm,
        "thr16": (np.arange(16, dtype=np.float32) * CBLK).reshape(1, 16),
        "iota128": np.arange(128, dtype=np.float32).reshape(128, 1),
        "framp": (np.arange(ST, dtype=np.float32) * 256).reshape(1, ST),
        "js24": np.arange(NBLK, dtype=np.float32).reshape(1, NBLK),
    }


def kernel(L0_attn_x, L0_attn_Wq, L0_attn_Wk, L0_attn_Wv,
           L0_ffn_qw, L0_ffn_qb, L0_ffn_kw, L0_ffn_kb, L0_ffn_vw, L0_ffn_vb,
           L0_ffn_ow, L0_ffn_ob, L0_ffn_gate_w, L0_ffn_w1, L0_ffn_w2,
           L0_ffn_norm1_w, L0_ffn_norm2_w, _phases=("A", "B", "C"), _taps=()):
    x = np.ascontiguousarray(np.asarray(L0_attn_x, dtype=np.float32))
    shared = {
        "wqT": np.ascontiguousarray(np.asarray(L0_attn_Wq).T),
        "wkT": np.ascontiguousarray(np.asarray(L0_attn_Wk).T),
        "wvT": np.ascontiguousarray(np.asarray(L0_attn_Wv).T),
        "qw": np.ascontiguousarray(np.asarray(L0_ffn_qw)[:, HEAD_PERM]),
        "kw": np.ascontiguousarray(np.asarray(L0_ffn_kw)),
        "vw": np.ascontiguousarray(np.asarray(L0_ffn_vw)),
        "ow": np.ascontiguousarray(np.asarray(L0_ffn_ow)[HEAD_PERM, :]),
        "qb": np.ascontiguousarray(np.asarray(L0_ffn_qb)[HEAD_PERM].reshape(8, 128).T),
        "qbr": np.ascontiguousarray(np.asarray(L0_ffn_qb)[HEAD_PERM].reshape(1, D)),
        "kbr": np.ascontiguousarray(np.asarray(L0_ffn_kb).reshape(1, 256)),
        "kb": np.ascontiguousarray(np.asarray(L0_ffn_kb).reshape(2, 128).T),
        "vbr": np.ascontiguousarray(np.asarray(L0_ffn_vb).reshape(1, 256)),
        "obr": np.ascontiguousarray(np.asarray(L0_ffn_ob).reshape(1, D)),
        "gw": np.ascontiguousarray(np.asarray(L0_ffn_gate_w).reshape(8, 128, 8).transpose(1, 0, 2).reshape(128, 64)),
        "w1": np.ascontiguousarray(np.asarray(L0_ffn_w1).reshape(NEXP * D, DFF)),
        "w2": np.ascontiguousarray(np.asarray(L0_ffn_w2).reshape(NEXP * DFF, D)),
        "n1wb": np.ascontiguousarray(np.broadcast_to(np.asarray(L0_ffn_norm1_w), (128, D))),
        "n2wb": np.ascontiguousarray(np.broadcast_to(np.asarray(L0_ffn_norm2_w), (128, D))),
    }
    nc = build_program(phases=_phases, taps=_taps)
    in_maps = []
    for c in range(NCORES):
        m = dict(shared)
        m["x"] = np.ascontiguousarray(x[c])
        m.update(make_aux(c))
        in_maps.append(m)
    res = run_bass_kernel_spmd(nc, in_maps, core_ids=list(range(NCORES)))
    out = np.stack([np.asarray(res.results[c]["out"]) for c in range(NCORES)])
    if _taps:
        tapres = [{t: np.asarray(res.results[c]["tap_" + t]) for t in _taps} for c in range(NCORES)]
        return out, tapres
    return out
